# version 90
# baseline (speedup 1.0000x reference)
import numpy as np
import ml_dtypes
import concourse.bass as bass
import concourse.mybir as mybir
from concourse.bass_utils import run_bass_kernel_spmd

F32 = mybir.dt.float32
BF16 = mybir.dt.bfloat16
I32 = mybir.dt.int32
I8 = mybir.dt.int8
OP = mybir.AluOpType
AF = mybir.ActivationFunctionType
AX = mybir.AxisListType

D = 1024
H = 16
CK = 128
KC = D // 128
NPAIR = H // 2
LN_EPS = 1e-5
A_EPS = 1e-6
A_SHIFT = 2.0

ENGINES = ("pe", "dve", "act", "pool", "sync")
NQ = 4
SEM_NAMES = ("pe", "dve", "act", "pool") + tuple(
    "%s%d" % (q, i) for q in ("qs", "qg") for i in range(NQ))


class Sched:

    def __init__(self):
        self.nodes = []
        self.reg = {}

    def op(self, eng, emit, reads=(), writes=(), dma=False, est=400):
        pbr = [r for r in reads if r.startswith("pb")]
        reads = [r for r in reads if not r.startswith("pb")]
        writes = list(writes) + pbr
        nid = len(self.nodes)
        preds = set()
        for r in reads:
            st = self.reg.get(r)
            if st and st["w"] is not None:
                preds.add(st["w"])
        for w in writes:
            st = self.reg.get(w)
            if st:
                if st["w"] is not None:
                    preds.add(st["w"])
                preds.update(st["r"])
        preds.discard(nid)
        self.nodes.append({"eng": eng, "emit": emit, "dma": dma, "est": est,
                           "preds": preds, "id": nid})
        for r in reads:
            st = self.reg.setdefault(r, {"w": None, "r": []})
            st["r"].append(nid)
        for w in writes:
            self.reg[w] = {"w": nid, "r": []}
        return nid

    def schedule(self):
        import heapq
        LAT = 320.5
        DMA_ISSUE = 1600.0
        n = len(self.nodes)
        succs = [[] for _ in range(n)]
        npred = [0] * n
        for nd in self.nodes:
            npred[nd["id"]] = len(nd["preds"])
            for p in nd["preds"]:
                succs[p].append(nd["id"])
        cp = [0] * n
        for i in range(n - 1, -1, -1):
            nd = self.nodes[i]
            best = 0
            for sc in succs[i]:
                v = cp[sc] + (LAT if self.nodes[sc]["eng"] != nd["eng"] else 0)
                if v > best:
                    best = v
            cp[i] = best + nd["est"]
        eng_clock = {e: 0.0 for e in ENGINES}
        finish = [0.0] * n
        future = {e: [] for e in ENGINES}
        runnable = {e: [] for e in ENGINES}
        for i, nd in enumerate(self.nodes):
            if npred[i] == 0:
                heapq.heappush(future[nd["eng"]], (0.0, -cp[i], i))
        order = {e: [] for e in ENGINES}
        gseq = []
        self.gseq = gseq
        scheduled = 0

        def promote(e, upto):
            f = future[e]
            while f and f[0][0] <= upto:
                es_, ncp_, i_ = heapq.heappop(f)
                heapq.heappush(runnable[e], (ncp_, es_, i_))

        while scheduled < n:
            bs, bcp, be = None, None, None
            for e in ENGINES:
                promote(e, eng_clock[e])
                if runnable[e]:
                    s = eng_clock[e]
                    c = -runnable[e][0][0]
                elif future[e]:
                    s = max(future[e][0][0], eng_clock[e])
                    c = -future[e][0][1]
                else:
                    continue
                if bs is None or s < bs - 1e-9 or (s < bs + 1e-9 and c > bcp):
                    bs, bcp, be = s, c, e
            e = be
            promote(e, bs)
            if runnable[e]:
                ncp_, es_, i = heapq.heappop(runnable[e])
            else:
                es_, ncp_, i = heapq.heappop(future[e])
            nd = self.nodes[i]
            start = max(bs, es_)
            busy = min(nd["est"], DMA_ISSUE) if nd["dma"] else nd["est"]
            eng_clock[e] = start + busy
            finish[i] = start + nd["est"]
            order[e].append(i)
            gseq.append(i)
            scheduled += 1
            for sc in succs[i]:
                npred[sc] -= 1
                if npred[sc] == 0:
                    nsc = self.nodes[sc]
                    es = 0.0
                    for p in nsc["preds"]:
                        v = finish[p] + (LAT if self.nodes[p]["eng"] != nsc["eng"]
                                         else 0.0)
                        if v > es:
                            es = v
                    heapq.heappush(future[nsc["eng"]], (es, -cp[sc], sc))
        return order


def _emit_scheduled(nc, S, extra_final=(), trunc=None):
    import contextlib
    order = S.schedule()
    nodes = S.nodes
    if trunc is not None:
        pos = {}
        k = 0
        flat = []
        iters = {e: iter(order[e]) for e in ENGINES}
        gseq = S.gseq
        keep = set(gseq[:trunc])
        order = {e: [i for i in order[e] if i in keep] for e in ENGINES}
    cnt = {s: 0 for s in SEM_NAMES}
    qn = {"qs": 0, "qg": 0}
    info = {}
    for eng in ENGINES:
        for i in order[eng]:
            nd = nodes[i]
            if nd["dma"]:
                q = {"sync": "qs", "pool": "qg"}[eng]
                k = qn[q]
                qn[q] += 1
                sem = "%s%d" % (q, k % NQ)
                step = 16
            else:
                sem = eng
                step = 1
            cnt[sem] += step
            info[i] = (sem, cnt[sem], step)
    stack = contextlib.ExitStack()
    sems = {s: stack.enter_context(nc.semaphore("sem_" + s)) for s in SEM_NAMES}
    blk = stack.enter_context(nc.Block())

    def runner(ename):
        def run(eng):
            seen = {}
            for i in order[ename]:
                nd = nodes[i]
                sem, mycnt, step = info[i]
                waits = {}
                if nd["dma"] and mycnt > 16:
                    waits[sem] = mycnt - 16
                for p in nd["preds"]:
                    psem, pcnt, _ = info[p]
                    if psem == "pe" and ename == "pe":
                        continue
                    if waits.get(psem, 0) < pcnt:
                        waits[psem] = pcnt
                for s_, c_ in waits.items():
                    if c_ > seen.get(s_, 0):
                        seen[s_] = c_
                        eng.wait_ge(sems[s_], c_)
                nd["emit"](eng).then_inc(sems[sem], step)
            for s_, c_ in extra_final.get(ename, []):
                eng.wait_ge(sems[s_], cnt[s_] if c_ is None else c_)
        return run

    blk.sync(runner("sync"))
    blk.tensor(runner("pe"))
    blk.vector(runner("dve"))
    blk.scalar(runner("act"))
    blk.gpsimd(runner("pool"))
    stack.close()
    return nc


DBG_TENSORS = []


def build_program(nseq, spc, debug_tc=None):
    ntc = nseq * spc
    NT = ntc * CK
    nc = bass.Bass("TRN2", target_bir_lowering=False)

    x_h = nc.dram_tensor("x", [NT, D], F32, kind="ExternalInput").ap()
    y_h = nc.dram_tensor("y", [NT, D], F32, kind="ExternalOutput").ap()
    wq_h = nc.dram_tensor("wq", [D, D], F32, kind="ExternalInput").ap()
    wk_h = nc.dram_tensor("wk", [D, D], F32, kind="ExternalInput").ap()
    wv_h = nc.dram_tensor("wv", [D, D], F32, kind="ExternalInput").ap()
    wo_h = nc.dram_tensor("wo", [D, D], F32, kind="ExternalInput").ap()
    wa_h = nc.dram_tensor("wa", [D, H], F32, kind="ExternalInput").ap()
    ba_h = nc.dram_tensor("ba", [H], F32, kind="ExternalInput").ap()
    rhs_h = nc.dram_tensor("rhs", [D, D], F32, kind="ExternalInput").ap()
    idb_h = nc.dram_tensor("idb", [128, 128], BF16, kind="ExternalInput").ap()
    i16_h = nc.dram_tensor("i16", [H, H], F32, kind="ExternalInput").ap()
    idf_h = nc.dram_tensor("idf", [128, 128], F32, kind="ExternalInput").ap()
    mask_h = nc.dram_tensor("mask4", [128, 512], BF16, kind="ExternalInput").ap()
    selhi_h = nc.dram_tensor("selhi", [1, 128], F32, kind="ExternalInput").ap()
    sello_h = nc.dram_tensor("sello", [1, 128], F32, kind="ExternalInput").ap()
    ones1_h = nc.dram_tensor("ones1", [1, 128], F32, kind="ExternalInput").ap()

    sb = {}

    def T(name, shape, dt):
        sb[name] = nc.alloc_sbuf_tensor("s_" + name, list(shape), dt).ap()
        return sb[name]

    T("Wq", (128, KC * 1024), BF16)
    T("Wk", (128, KC * 1024), BF16)
    T("Wv", (128, KC * 1024), BF16)
    T("Wa", (128, KC * H), BF16)
    T("Wo", (128, KC * 1025), BF16)
    T("QR", (128, KC * 1025), BF16)
    T("idb", (128, 128), BF16)
    T("i16", (H, H), F32)
    T("idf", (128, 128), F32)
    T("mask4", (128, 512), BF16)
    T("selhi", (1, 128), F32)
    T("sello", (1, 128), F32)
    T("ones1", (1, 128), F32)
    T("zeros16", (H, CK), F32)
    T("xs4", (128, 4 * D), F32)
    T("x2d", (128, 2 * D), F32)
    for pf in range(2):
        s = str(pf)
        T("xn" + s, (128, D), BF16)
        T("xnf" + s, (128, D), BF16)
        T("qpre" + s, (128, D), BF16)
        T("ktok" + s, (128, D), BF16)
        T("vz" + s, (128, H * 65), BF16)
        T("qAB" + s, (128, 2 * D), BF16)
        T("kfm" + s, (128, D), BF16)
        T("kapT" + s, (128, H), F32)
        T("rep" + s, (128, 24), F32)
        T("lu" + s, (H, CK), F32)
        T("le" + s, (H, CK), F32)
        T("ll" + s, (H, CK), F32)
        T("lcum" + s, (H, CK), F32)
        T("karg" + s, (H, CK), F32)
        T("ext" + s, (H, 4), F32)
        T("prow" + s, (1, 48), F32)
    T("scm", (128, H * 128), BF16)
    T("sqj", (128, D), BF16)
    T("sqk", (128, D), BF16)
    T("e_sb", (128, D), BF16)
    T("r_sb", (128, D), BF16)
    T("otok", (128, D), BF16)
    T("ofm", (128, D), BF16)
    T("o_n", (128, D), F32)
    T("o_f", (128, D), F32)
    T("ffn_f", (128, 2 * D), F32)
    T("S0", (128, NPAIR * 65), F32)
    T("S1", (128, NPAIR * 65), F32)
    T("Szb", (128, NPAIR * 65), BF16)
    T("Pg_all", (128, NPAIR * 65), F32)
    T("qi", (128, D), I8)
    T("gq", (128, D), BF16)
    T("ql", (128, D), BF16)
    T("ql2", (128, D), BF16)
    T("gq2", (128, D), BF16)
    T("qlf2", (128, D), BF16)
    T("qlf", (128, D), BF16)
    T("fn", (128, D), F32)
    T("wrep", (128, D), F32)
    T("den_all", (128, H), F32)
    T("denf", (128, H), F32)
    T("recip", (128, H), F32)
    T("rsT8", (128, KC), F32)
    T("ecol", (128, 1), F32)
    T("ba2", (H, 1), F32)
    T("st", (128, 120), F32)
    T("cst", (128, 4), F32)
    c_eps = sb["cst"][:, 0:1]
    c_neg1 = sb["cst"][:, 1:2]
    c_negh = sb["cst"][:, 2:3]
    c_one = sb["cst"][:, 3:4]

    ST = {}
    _stn = [0]

    def stc(name):
        if name not in ST:
            ST[name] = _stn[0]
            _stn[0] += 1
            assert _stn[0] <= 120
        i = ST[name]
        return sb["st"][:, i:i + 1], "st_" + name

    pbs = [nc.alloc_psum_tensor("pbs%d" % i, [128, 1024], F32).ap()
           for i in range(3)]
    pb = [pbs[i // 2][:, (i % 2) * 512:(i % 2 + 1) * 512] for i in range(6)]
    pb.append(nc.alloc_psum_tensor("pb6", [128, 512], F32).ap())
    pb.append(nc.alloc_psum_tensor("pb7", [128, 512], F32).ap())

    S = Sched()

    def dma(q, out, in_, reads=(), writes=()):
        S.op(q, lambda e, o=out, i=in_: e.dma_start(out=o, in_=i),
             reads=reads, writes=writes, dma=True, est=2500)

    assert nseq == 2

    def row0(tc):
        return (tc % 2) * spc * CK + (tc // 2) * CK

    dma("sync", sb["idb"][:, :], idb_h, writes=["idb"])
    dma("sync", sb["i16"][:, :], i16_h, writes=["i16"])
    dma("sync", sb["idf"][:, :], idf_h, writes=["idf"])
    dma("sync", sb["mask4"][:, :], mask_h, writes=["mask4"])
    dma("sync", sb["selhi"][:, :], selhi_h, writes=["selhi"])
    dma("sync", sb["sello"][:, :], sello_h, writes=["sello"])
    dma("sync", sb["ones1"][:, :], ones1_h, writes=["ones1"])
    dma("sync", sb["ba2"][:, :], ba_h.rearrange("(p o) -> p o", o=1), writes=["ba2"])
    def load_x(slot, r0):
        dma("sync", sb["xs4"][:, slot * D:slot * D + 512],
            x_h[r0:r0 + CK, 0:512], writes=["xs%d" % slot])
        dma("pool", sb["xs4"][:, slot * D + 512:(slot + 1) * D],
            x_h[r0:r0 + CK, 512:1024], writes=["xsh%d" % slot])

    for i0 in range(min(4, ntc)):
        load_x(i0, row0(i0))
    for nm, hb in (("Wq", wq_h), ("Wk", wk_h), ("Wv", wv_h)):
        tgt = sb[nm].rearrange("p (c d) -> p c d", c=KC)
        srcr = hb.rearrange("(c p) d -> p c d", p=128)
        dma("pool", tgt[:, 0:KC // 2, :], srcr[:, 0:KC // 2, :], writes=[nm])
        dma("pool", tgt[:, KC // 2:KC, :], srcr[:, KC // 2:KC, :],
            writes=[nm + "h"])
    dma("pool", sb["Wa"].rearrange("p (c d) -> p c d", c=KC),
        wa_h.rearrange("(c p) d -> p c d", p=128), writes=["Wa"])
    tgt = sb["Wo"].rearrange("p (c d) -> p c d", c=KC, d=1025)[:, :, 0:1024]
    srcr = wo_h.rearrange("(c p) d -> p c d", p=128)
    dma("pool", tgt[:, 0:KC // 2, :], srcr[:, 0:KC // 2, :], writes=["Wo"])
    dma("pool", tgt[:, KC // 2:KC, :], srcr[:, KC // 2:KC, :], writes=["Woh"])

    for pf in range(2):
        s = str(pf)
        S.op("pool", lambda e, s=s: e.memset(
            sb["vz" + s].rearrange("p (h c) -> p h c", c=65)[:, :, 64:65], 1.0),
            writes=["vz_ones%s" % s])
    S.op("pool", lambda e: e.memset(sb["zeros16"][:, :], 0.0), writes=["zeros16"])
    for pf in range(2):
        s = str(pf)
        S.op("pool", lambda e, s=s: e.memset(sb["qAB" + s][:, :], 0.0),
             writes=["qAB%s_pad" % s])
    for i, v in ((0, LN_EPS), (1, -1.0), (2, -0.5), (3, 1.0)):
        S.op("pool", lambda e, i=i, v=v: e.memset(sb["cst"][:, i:i + 1], v),
             writes=["cst%d" % i])
    S.op("dve", lambda e: e.tensor_scalar_add(sb["ba2"][:, :], sb["ba2"][:, :], A_SHIFT),
         reads=["ba2"], writes=["ba2"])
    for ld in range(KC):
        buf = "o_n" if ld % 2 == 0 else "fn"
        dma("sync" if ld % 2 == 0 else "pool", sb[buf][:, :],
            rhs_h.rearrange("(c p) d -> p c d", p=128)[:, ld, :],
            writes=[buf])
        if ld == 0:
            S.op("dve", lambda e, buf=buf: e.tensor_copy(
                sb["o_f"][:, :], sb[buf][:, :]),
                reads=[buf], writes=["qt_f"])
            S.op("dve", lambda e, buf=buf: e.tensor_copy(
                sb["x2d"][:, 1024:2048], sb[buf][:, :]),
                reads=[buf], writes=["x2_1"])
        else:
            S.op("dve", lambda e, buf=buf: e.tensor_tensor(
                sb["o_f"][:, :], sb["o_f"][:, :], sb[buf][:, :], OP.max),
                reads=["qt_f", buf], writes=["qt_f"])
            S.op("dve", lambda e, buf=buf: e.tensor_tensor(
                sb["x2d"][:, 1024:2048], sb["x2d"][:, 1024:2048], sb[buf][:, :],
                OP.min),
                reads=["x2_1", buf], writes=["x2_1"])
    S.op("dve", lambda e: e.tensor_scalar_mul(sb["x2d"][:, 1024:2048], sb["x2d"][:, 1024:2048], -1.0),
         reads=["x2_1"], writes=["x2_1"])
    S.op("dve", lambda e: e.tensor_tensor(
        sb["o_f"][:, :], sb["o_f"][:, :], sb["x2d"][:, 1024:2048], OP.max),
        reads=["qt_f", "x2_1"], writes=["qt_f"])
    S.op("dve", lambda e: e.tensor_copy(sb["sqj"][:, :], sb["o_f"][:, :]),
         reads=["qt_f"], writes=["sqj"])
    for c in range(KC):
        S.op("pe", lambda e, c=c: e.matmul(
            pb[c // 4][:, (c % 4) * 128:(c % 4) * 128 + 128],
            sb["sqj"][:, c * 128:(c + 1) * 128], sb["idb"][:, :],
            start=True, stop=True),
            reads=["sqj", "idb"], writes=["pb%d" % (c // 4)])
    for c in range(KC):
        S.op("dve", lambda e, c=c: e.reduce_max(
            sb["rsT8"][:, c:c + 1], pb[c // 4][:, (c % 4) * 128:(c % 4) * 128 + 128],
            AX.X), reads=["pb%d" % (c // 4)], writes=["rsT8"])
    S.op("dve", lambda e: e.tensor_scalar(
        sb["rsT8"][:, :], sb["rsT8"][:, :], 1e-12, 1.0 / 127.0, OP.max, OP.mult),
        reads=["rsT8"], writes=["rsT8"])
    for half in range(2):
        def em_rr(e, half=half):
            last = None
            for j in range(4):
                c = half * 4 + j
                last = e.matmul(pb[2 + half][0:1, j * 128:(j + 1) * 128],
                                sb["rsT8"][:, c:c + 1], sb["idf"][:, :],
                                start=True, stop=True)
            return last
        S.op("pe", em_rr, reads=["rsT8", "idf"], writes=["pb%d" % (2 + half)])
    for half in range(2):
        S.op("dve", lambda e, half=half: e.tensor_copy(
            sb["x2d"][0:1, 0:1024][:, half * 512:(half + 1) * 512], pb[2 + half][0:1, :]),
            reads=["pb%d" % (2 + half)], writes=["x2_0" if half else "x2_1"])
    for half in range(2):
        S.op("pe", lambda e, half=half: e.matmul(
            pb[4 + half][:, :], sb["ones1"][:, :],
            sb["x2d"][0:1, 0:1024][:, half * 512:(half + 1) * 512], start=True, stop=True),
            reads=["x2_0", "x2_1", "ones1"], writes=["pb%d" % (4 + half)])
    for half in range(2):
        S.op("dve", lambda e, half=half: e.tensor_copy(
            sb["wrep"][:, half * 512:(half + 1) * 512], pb[4 + half][:, :]),
            reads=["pb%d" % (4 + half)], writes=["wrep" if half else "wrep_h0"])
    S.op("dve", lambda e: e.reciprocal(sb["fn"][:, :], sb["wrep"][:, :]),
         reads=["wrep", "wrep_h0"], writes=["fn"])

    for c in range(KC):
        buf, breg = (("o_n", "o_n") if c % 2 == 0
                     else ("x2d", "x2_1"))
        bap = sb["o_n"][:, :] if c % 2 == 0 else sb["x2d"][:, 1024:2048]
        dma("sync" if c % 2 == 0 else "pool", bap,
            rhs_h.rearrange("(c p) d -> p c d", p=128)[:, c, :],
            writes=[breg])
        S.op("dve", lambda e, bap=bap: e.tensor_tensor(
            sb["o_f"][:, :], bap, sb["fn"][:, :], OP.mult),
            reads=[breg, "fn"], writes=["qt_f"])
        S.op("dve", lambda e: e.tensor_scalar_add(
            sb["o_f"][:, :], sb["o_f"][:, :], 0.5),
            reads=["qt_f"], writes=["qt_f"])
        S.op("act", lambda e: e.activation(sb["qi"][:, :], sb["o_f"][:, :], AF.Copy),
             reads=["qt_f"], writes=["qi"])
        S.op("dve", lambda e: e.tensor_tensor(
            sb["gq"][:, :], sb["qi"][:, :], sb["o_f"][:, :], OP.is_gt),
            reads=["qi", "qt_f"], writes=["gq"], est=1180)
        S.op("dve", lambda e, c=c: e.scalar_tensor_tensor(
            sb["QR"][:, c * 1025:c * 1025 + 1024], sb["gq"][:, :], -1.0,
            sb["qi"][:, :], OP.mult, OP.add),
            reads=["gq", "qi"], writes=["QR_c%d" % c])
    for c in range(KC):
        S.op("dve", lambda e, c=c: e.tensor_tensor(
            sb["QR"][:, c * 1025:c * 1025 + 1024],
            sb["QR"][:, c * 1025:c * 1025 + 1024],
            sb["wrep"][:, :], OP.mult),
            reads=["QR_c%d" % c, "wrep", "wrep_h0"], writes=["QR_c%d" % c],
            est=1180)

    def ln_scalars(tag, sum_reg, sumsq_reg, sum_ap, sumsq_ap, n=D, adj=None):
        m_ap, m_reg = stc(tag + "_m")
        exq_ap, exq_reg = stc(tag + "_exq")
        nv_ap, nv_reg = stc(tag + "_nv")
        rstd_ap, rstd_reg = stc(tag + "_rstd")
        lnv_ap, lnv_reg = stc(tag + "_lnv")
        bias_ap, bias_reg = stc(tag + "_bias")
        if adj is not None and ST[tag + "_exq"] == ST[tag + "_m"] + 1:
            m2 = sb["st"][:, ST[tag + "_m"]:ST[tag + "_m"] + 2]
            S.op("dve", lambda e, m2=m2, adj=adj: e.tensor_scalar_mul(
                m2, adj, 1.0 / n),
                reads=[sum_reg, sumsq_reg], writes=[m_reg, exq_reg])
        else:
            S.op("dve", lambda e: e.tensor_scalar_mul(m_ap, sum_ap, 1.0 / n),
                 reads=[sum_reg], writes=[m_reg])
            S.op("dve", lambda e: e.tensor_scalar_mul(exq_ap, sumsq_ap, 1.0 / n),
                 reads=[sumsq_reg], writes=[exq_reg])
        S.op("dve", lambda e: e.scalar_tensor_tensor(
            nv_ap, m_ap, m_ap, exq_ap, OP.mult, OP.subtract),
            reads=[m_reg, exq_reg], writes=[nv_reg])
        S.op("act", lambda e: e.activation(lnv_ap, nv_ap, AF.Ln, bias=c_eps,
                                           scale=c_neg1),
             reads=[nv_reg, "cst0", "cst1"], writes=[lnv_reg])
        S.op("act", lambda e: e.activation(rstd_ap, lnv_ap, AF.Exp, scale=c_negh),
             reads=[lnv_reg, "cst2"], writes=[rstd_reg])
        S.op("dve", lambda e: e.scalar_tensor_tensor(
            bias_ap, m_ap, -1.0, rstd_ap, OP.mult, OP.mult),
            reads=[m_reg, rstd_reg], writes=[bias_reg])
        return m_ap, m_reg, rstd_ap, rstd_reg, bias_ap, bias_reg


    ln1_infos = {}

    def front(tc):
        pf = tc % 2
        s = str(pf)
        xs = sb["xs4"][:, (tc % 4) * D:(tc % 4) * D + D]
        xs_reg = ["xs%d" % (tc % 4), "xsh%d" % (tc % 4)]

        s4 = str(tc % 4)
        sum1_ap, sum1_reg = stc("sum1" + s4)
        ssq1_ap, ssq1_reg = stc("ssq1" + s4)
        S.op("act", lambda e, xs=xs, a=sum1_ap: e.activation(
            sb["sqj"][:, :], xs[:, :], AF.Copy, accum_out=a),
            reads=xs_reg, writes=["sqj", sum1_reg], est=1100)
        S.op("act", lambda e, xs=xs, a=ssq1_ap, s=s: e.activation(
            sb["sqj"][:, :], xs[:, :], AF.Square, accum_out=a),
            reads=xs_reg, writes=["sqj", ssq1_reg], est=1100)
        adj1 = sb["st"][:, ST["sum1" + s4]:ST["sum1" + s4] + 2]
        m1_ap, m1_reg, rstd1_ap, rstd1_reg, bias1_ap, bias1_reg = ln_scalars(
            "ln1" + s4, sum1_reg, ssq1_reg, sum1_ap, ssq1_ap, adj=adj1)
        S.op("act", lambda e, xs=xs, s=s, b=bias1_ap, sc=rstd1_ap: e.activation(
            sb["xn" + s][:, :], xs[:, :], AF.Identity, bias=b, scale=sc),
            reads=xs_reg + [rstd1_reg, bias1_reg], writes=["xn" + s], est=1100)
        ln1_infos[tc] = (sum1_ap, sum1_reg, m1_ap, m1_reg)

        S.op("sync", lambda e, s=s: e.dma_start(
            out=sb["xnf" + s].rearrange("p (c t) -> p c t", t=CK),
            in_=sb["xn" + s][:, :], transpose=True),
            reads=["xn" + s], writes=["xnf%s_h0" % s, "xnf" + s],
            dma=True, est=2500)
        xnf_regs = ["xnf" + s, "xnf%s_h0" % s]

        def em_a(e, s=s):
            last = None
            for kc in range(KC):
                last = e.matmul(pb[7][0:H, 0:CK],
                                sb["Wa"][:, kc * H:(kc + 1) * H],
                                sb["xnf" + s][:, kc * 128:(kc + 1) * 128],
                                start=(kc == 0), stop=(kc == KC - 1))
            return last
        S.op("pe", em_a, reads=xnf_regs + ["Wa", "pb7"], writes=["pb7"], est=500)

        S.op("dve", lambda e, s=s: e.tensor_scalar(
            sb["lu" + s][:, :], pb[7][0:H, 0:CK], sb["ba2"][:, :], None, OP.add),
            reads=["pb7", "ba2"], writes=["lu" + s])
        S.op("act", lambda e, s=s: e.activation(sb["le" + s][:, :],
                                                sb["lu" + s][:, :], AF.Abs),
             reads=["lu" + s], writes=["le" + s])
        S.op("act", lambda e, s=s: e.activation(sb["le" + s][:, :],
                                                sb["le" + s][:, :],
                                                AF.Exp, scale=sb["cst"][0:H, 1:2]),
             reads=["le" + s, "cst1"], writes=["le" + s])
        S.op("act", lambda e, s=s: e.activation(sb["le" + s][:, :],
                                                sb["le" + s][:, :],
                                                AF.Ln, bias=sb["cst"][0:H, 3:4]),
             reads=["le" + s, "cst3"], writes=["le" + s])
        S.op("dve", lambda e, s=s: e.scalar_tensor_tensor(
            sb["ll" + s][:, :], sb["lu" + s][:, :], 0.0, sb["le" + s][:, :],
            OP.min, OP.subtract),
            reads=["lu" + s, "le" + s], writes=["ll" + s])
        S.op("dve", lambda e, s=s: e.tensor_tensor_scan(
            sb["lcum" + s][:, :], sb["ll" + s][:, :], sb["zeros16"][:, :], 0.0,
            OP.add, OP.add),
            reads=["ll" + s, "zeros16"], writes=["lcum" + s])
        S.op("dve", lambda e, s=s: e.tensor_copy(
            sb["ext" + s][:, 0:2], sb["lcum" + s][:, 63:128:64]),
            reads=["lcum" + s], writes=["ext0" + s])
        S.op("dve", lambda e, s=s: e.tensor_sub(
            sb["ext" + s][:, 2:3], sb["ext" + s][:, 1:2],
            sb["ext" + s][:, 0:1]),
            reads=["ext0" + s], writes=["ext2" + s])
        S.op("act", lambda e, s=s: e.activation(
            sb["ext" + s][:, 0:3], sb["ext" + s][:, 0:3], AF.Exp),
            reads=["ext0" + s, "ext2" + s],
            writes=["ext0" + s, "ext2" + s])

        S.op("dve", lambda e, s=s: e.tensor_scalar(
            sb["karg" + s][:, :], sb["lcum" + s][:, :], sb["lcum" + s][:, 63:64],
            -1.0, OP.subtract, OP.mult),
            reads=["lcum" + s], writes=["karg" + s])
        S.op("dve", lambda e, s=s: e.tensor_scalar_min(
            sb["karg" + s][:, :], sb["karg" + s][:, :], 80.0),
            reads=["karg" + s], writes=["karg" + s])

        yield
        def proj_group(W, gidx, bank_i):
            def em(e, W=W, gidx=gidx, bank_i=bank_i, s=s):
                last = None
                for kc in range(KC):
                    last = e.matmul(
                        pb[bank_i][:, :],
                        sb["xnf" + s][:, kc * 128:(kc + 1) * 128],
                        sb[W][:, kc * 1024 + gidx * 512:kc * 1024 + (gidx + 1) * 512],
                        start=(kc == 0), stop=(kc == KC - 1))
                return last
            S.op("pe", em, reads=xnf_regs + [W, W + "h"],
                 writes=["pb%d" % bank_i], est=1900)

        def elu_evac_k(dst):
            S.op("act", lambda e: e.activation(
                sb["e_sb"][:, 0:512], pb[2][:, :], AF.Exp),
                reads=["pb2"], writes=["e_h0"], est=685)
            S.op("act", lambda e: e.activation(
                sb["r_sb"][:, 512:1024], pb[3][:, :], AF.Relu),
                reads=["pb3"], writes=["r_sb"], est=685)
            S.op("act", lambda e: e.activation(
                sb["e_sb"][:, 512:1024], pb[3][:, :], AF.Exp),
                reads=["pb3"], writes=["e_sb"], est=685)
            S.op("act", lambda e: e.activation(
                sb["r_sb"][:, 0:512], pb[2][:, :], AF.Relu),
                reads=["pb2"], writes=["r_h0"], est=685)
            S.op("dve", lambda e, dst=dst: e.tensor_scalar_min(
                sb[dst][:, :], sb["e_sb"][:, :], 1.0),
                reads=["e_sb", "e_h0"], writes=[dst], est=340)
            S.op("dve", lambda e, dst=dst: e.tensor_add(
                sb[dst][:, :], sb[dst][:, :], sb["r_sb"][:, :]),
                reads=[dst, "r_sb", "r_h0"], writes=[dst], est=620)

        def elu_evac(dst):
            S.op("act", lambda e: e.activation(
                sb["e_sb"][:, 0:512], pb[0][:, :], AF.Exp),
                reads=["pb0"], writes=["e_h0"], est=685)
            S.op("act", lambda e: e.activation(
                sb["r_sb"][:, 512:1024], pb[1][:, :], AF.Relu),
                reads=["pb1"], writes=["r_sb"], est=685)
            S.op("act", lambda e: e.activation(
                sb["e_sb"][:, 512:1024], pb[1][:, :], AF.Exp),
                reads=["pb1"], writes=["e_sb"], est=685)
            S.op("act", lambda e: e.activation(
                sb["r_sb"][:, 0:512], pb[0][:, :], AF.Relu),
                reads=["pb0"], writes=["r_h0"], est=685)
            S.op("dve", lambda e, dst=dst: e.tensor_scalar_min(
                sb[dst][:, :], sb["e_sb"][:, :], 1.0),
                reads=["e_sb", "e_h0"], writes=[dst], est=340)
            S.op("dve", lambda e, dst=dst: e.tensor_add(
                sb[dst][:, :], sb[dst][:, :], sb["r_sb"][:, :]),
                reads=[dst, "r_sb", "r_h0"], writes=[dst], est=620)

        proj_group("Wq", 0, 0)
        proj_group("Wq", 1, 1)
        proj_group("Wk", 0, 2)
        proj_group("Wk", 1, 3)
        elu_evac("qpre" + s)
        proj_group("Wv", 0, 0)
        proj_group("Wv", 1, 1)
        elu_evac_k("ktok" + s)
        out_ap = sb["vz" + s][:, :].rearrange(
            "p (h c) -> p h c", c=65)[:, :, 0:64]
        in_ap = pbs[0].rearrange("p (h c) -> p h c", c=64)
        S.op("act", lambda e, o=out_ap, i=in_ap: e.activation(o, i, AF.Copy),
             reads=["pb0", "pb1", "vz_ones%s" % s],
             writes=["vz" + s, "vz%s_h0" % s, "vzk" + s], est=1180)

        yield
        def em_rm(e, s=s):
            last = None
            for j in range(3):
                last = e.matmul(pb[7][0:1, 128 + j * 16:128 + (j + 1) * 16],
                                sb["ext" + s][:, j:j + 1], sb["i16"][:, :],
                                start=True, stop=True)
            return last
        S.op("pe", em_rm, reads=["ext0" + s, "ext2" + s, "i16", "pb7"],
             writes=["pb7"])
        S.op("act", lambda e, s=s: e.activation(sb["prow" + s][:, :],
                                                pb[7][0:1, 128:176], AF.Copy),
             reads=["pb7"], writes=["prow" + s])

        def em_pr(e, s=s):
            last = None
            for j in range(3):
                rr = sb["prow" + s][0:1, j * 16:(j + 1) * 16]
                last = e.matmul(pb[7][:, 176 + j * 8:176 + (j + 1) * 8],
                                sb["selhi"][:, :], rr[:, 0::2],
                                start=True, stop=False)
                last = e.matmul(pb[7][:, 176 + j * 8:176 + (j + 1) * 8],
                                sb["sello"][:, :], rr[:, 1::2],
                                start=False, stop=True)
            return last
        S.op("pe", em_pr, reads=["prow" + s, "selhi", "sello", "pb7"],
             writes=["pb7"])
        S.op("act", lambda e, s=s: e.activation(sb["rep" + s][:, :],
                                                pb[7][:, 176:200], AF.Copy),
             reads=["pb7"], writes=["rep" + s])

        S.op("pe", lambda e, s=s: e.matmul(pb[7][:, 200:216], sb["karg" + s][:, :],
                                           sb["i16"][:, :], start=True, stop=True),
             reads=["karg" + s, "i16", "pb7"], writes=["pb7"])
        S.op("act", lambda e, s=s: e.activation(sb["kapT" + s][:, :],
                                                pb[7][:, 200:216], AF.Exp),
             reads=["pb7"], writes=["kapT" + s])

        kexp = sb["kapT" + s].rearrange("p (h o) -> p h o", o=1).to_broadcast(
            [128, H, 64])
        vzv = sb["vz" + s][:, :].rearrange("p (h c) -> p h c", c=65)[:, :, 0:64]
        S.op("dve", lambda e, vzv=vzv, kexp=kexp: e.tensor_tensor(
            vzv, vzv, kexp, OP.mult),
            reads=["vz" + s, "vz%s_h0" % s, "kapT" + s],
            writes=["vzk" + s], est=1180)
        vzz = sb["vz" + s][:, :].rearrange("p (h c) -> p h c", c=65)[:, :, 64:65]
        S.op("dve", lambda e, vzz=vzz, s=s: e.tensor_copy(
            vzz, sb["kapT" + s].rearrange("p (h o) -> p h o", o=1)),
            reads=["kapT" + s, "vz_ones%s" % s],
            writes=["vzk" + s], est=120)

        yield
        S.op("sync", lambda e, s=s: e.dma_start(
            out=sb["kfm" + s].rearrange("p (c t) -> p c t", t=CK),
            in_=sb["ktok" + s][:, :], transpose=True),
            reads=["ktok" + s], writes=["kfm%s_h0" % s, "kfm" + s],
            dma=True, est=2500)
        for srcn, dst in (("qpre", "q"),):
            for half in range(2):
                def em_t(e, srcn=srcn, half=half, s=s):
                    last = None
                    for j in range(4):
                        p_ = half * 4 + j
                        last = e.matmul(pb[half][:, j * 128:(j + 1) * 128],
                                        sb[srcn + s][:, p_ * 128:(p_ + 1) * 128],
                                        sb["idb"][:, :], start=True, stop=True)
                    return last
                S.op("pe", em_t, reads=[srcn + s, "idb"], writes=["pb%d" % half], est=550)
            if dst == "k":
                pass
            else:
                def qout(half, side, s=s):
                    base = 1024 * half
                    v = sb["qAB" + s][side * 64:(side + 1) * 64,
                                      base:base + 1024]
                    return v.rearrange("p (q two c) -> p q two c", two=2,
                                       c=128)[:, :, side, :]
                def qin(half, side):
                    return pb[half][side * 64:(side + 1) * 64, :].rearrange(
                        "p (q c) -> p q c", c=128)
                S.op("dve", lambda e, o=qout(0, 0), i=qin(0, 0): e.tensor_copy(o, i),
                     reads=["pb0", "qAB%s_pad" % s], writes=["qAB%s_a0" % s], est=685)
                S.op("dve", lambda e, o=qout(0, 1), i=qin(0, 1): e.tensor_copy(o, i),
                     reads=["pb0", "qAB%s_pad" % s], writes=["qAB%s_b0" % s], est=685)
                S.op("act", lambda e, o=qout(1, 0), i=qin(1, 0): e.activation(
                    o, i, AF.Copy),
                    reads=["pb1", "qAB%s_pad" % s], writes=["qAB%s_a1" % s], est=685)
                S.op("act", lambda e, o=qout(1, 1), i=qin(1, 1): e.activation(
                    o, i, AF.Copy),
                    reads=["pb1", "qAB%s_pad" % s], writes=["qAB%s_b1" % s], est=685)


    def scan_chunk(tc):
        pf = tc % 2
        s = str(pf)
        cidx = tc // 2
        St = "S" + s
        qA = ["qAB%s_%s" % (s, t) for t in ("a0", "b0", "a1", "b1")] + \
            ["qAB%s_pad" % s]
        kf = ["kfm" + s, "kfm%s_h0" % s]
        vzr = ["vzk" + s]

        if cidx == 0:
            S.op("dve", lambda e, St=St: e.memset(sb[St][:, :], 0.0),
                 reads=[St], writes=[St])

        for w in range(2):
            for p in range(4 * w, 4 * w + 4):
                bank = 2 + (p % 4) // 2
                col = (p % 2) * 256

                def em_sc(e, p=p, bank=bank, col=col, s=s):
                    return e.matmul(pb[bank][:, col:col + 256],
                                    sb["kfm" + s][:, p * 128:(p + 1) * 128],
                                    sb["qAB" + s][:, p * 256:(p + 1) * 256],
                                    start=True, stop=True)
                S.op("pe", em_sc, reads=kf + qA, writes=["pb%d" % bank], est=220)
            for b in range(2):
                q = 2 * w + b
                S.op("dve", lambda e, b=b, q=q: e.tensor_tensor(
                    sb["scm"][:, q * 512:(q + 1) * 512], pb[2 + b][:, :],
                    sb["mask4"][:, :], OP.mult),
                    reads=["pb%d" % (2 + b), "mask4"], writes=["scm_b%d" % q],
                    est=685)

        eexp = sb["rep" + s][:, 0:8].rearrange(
            "p (q o) -> p q o", o=1).to_broadcast([128, NPAIR, 65])
        S.op("dve", lambda e, eexp=eexp, St=St: e.tensor_tensor(
            sb["Szb"][:, :].rearrange("p (q c) -> p q c", c=65),
            sb[St][:, :].rearrange("p (q c) -> p q c", c=65), eexp, OP.mult),
            reads=[St, "rep" + s], writes=["Szb"], est=650)

        for p in range(NPAIR):
            bank = 2 + p // 4
            col = (p % 4) * 65

            def em_P(e, p=p, bank=bank, col=col, s=s):
                hA, hB = 2 * p, 2 * p + 1
                e.matmul(pb[bank][0:64, col:col + 65],
                         sb["ktok" + s][:, hA * 64:(hA + 1) * 64],
                         sb["vz" + s][:, hA * 65:(hA + 1) * 65],
                         start=True, stop=True)
                return e.matmul(pb[bank][64:128, col:col + 65],
                                sb["ktok" + s][:, hB * 64:(hB + 1) * 64],
                                sb["vz" + s][:, hB * 65:(hB + 1) * 65],
                                start=True, stop=True, tile_position=(0, 64))
            S.op("pe", em_P, reads=["ktok" + s] + vzr, writes=["pb%d" % bank],
                 est=150)

        for bank, p0 in ((2, 0), (3, 4)):
            gexp = sb["rep" + s][:, 16 + p0:20 + p0].rearrange(
                "p (q o) -> p q o", o=1).to_broadcast([128, 4, 65])
            S.op("dve", lambda e, bank=bank, p0=p0, g=gexp: e.tensor_tensor(
                sb["Pg_all"][:, p0 * 65:(p0 + 4) * 65].rearrange(
                    "p (q c) -> p q c", c=65),
                pb[bank][:, 0:260].rearrange("p (q c) -> p q c", c=65), g, OP.mult),
                reads=["pb%d" % bank, "rep" + s], writes=["Pg_%d" % bank], est=340)
        NMB = (2, 2, 2, 3, 3, 3, 6, 6)
        for p in range(NPAIR):
            nm_bank = NMB[p]
            col = (p % 3) * 130

            def em_num(e, p=p, nm_bank=nm_bank, col=col, s=s):
                hA, hB = 2 * p, 2 * p + 1
                e.matmul(pb[nm_bank][:, col:col + 65],
                         sb["scm"][:, p * 256:p * 256 + 128],
                         sb["vz" + s][:, hA * 65:(hA + 1) * 65],
                         start=True, stop=False)
                e.matmul(pb[nm_bank][:, col:col + 65],
                         sb["qAB" + s][:, p * 256:p * 256 + 128],
                         sb["Szb"][:, p * 65:(p + 1) * 65], start=False, stop=True)
                e.matmul(pb[nm_bank][:, col + 65:col + 130],
                         sb["scm"][:, p * 256 + 128:p * 256 + 256],
                         sb["vz" + s][:, hB * 65:(hB + 1) * 65],
                         start=True, stop=False)
                return e.matmul(pb[nm_bank][:, col + 65:col + 130],
                                sb["qAB" + s][:, p * 256 + 128:(p + 1) * 256],
                                sb["Szb"][:, p * 65:(p + 1) * 65],
                                start=False, stop=True)
            S.op("pe", em_num,
                 reads=["scm_b%d" % (p // 2), "Szb"] + vzr + qA,
                 writes=["pb%d" % nm_bank], est=280)

        NP_BANK = ((2, 0, 3), (3, 3, 3), (6, 6, 2))
        for bank, p0, npr in NP_BANK:
            in_ap = pb[bank][:, 0:npr * 130].rearrange(
                "p (q x) -> p q x", x=130)[:, :, 64:130:65]
            out_ap = sb["den_all"][:, 2 * p0:2 * (p0 + npr)].rearrange(
                "p (a b) -> p a b", b=2)
            S.op("act", lambda e, o=out_ap, i=in_ap: e.activation(o, i, AF.Copy),
                 reads=["pb%d" % bank], writes=["den_b%d" % bank])
        S.op("dve", lambda e, s=s: e.scalar_tensor_tensor(
            sb["denf"][:, :], sb["kapT" + s][:, :], A_EPS, sb["den_all"][:, :],
            OP.mult, OP.add),
            reads=["kapT" + s, "den_b2", "den_b3", "den_b6"], writes=["denf"])
        S.op("dve", lambda e: e.reciprocal(sb["recip"][:, :], sb["denf"][:, :]),
             reads=["denf"], writes=["recip"])

        for bank, p0, npr in NP_BANK:
            for side in range(2):
                num_ap = pb[bank][:, 0:npr * 130].rearrange(
                    "p (q x) -> p q x", x=130)[:, :, side * 65:side * 65 + 64]
                rexp = sb["recip"][:, 2 * p0 + side:2 * (p0 + npr):2].rearrange(
                    "p (q o) -> p q o", o=1).to_broadcast([128, npr, 64])
                out_ap = sb["otok"][:, p0 * 128:(p0 + npr) * 128].rearrange(
                    "p (q t c) -> p q t c", t=2, c=64)[:, :, side, :]
                S.op("dve", lambda e, o=out_ap, i=num_ap, r=rexp: e.tensor_tensor(
                    o, i, r, OP.mult),
                    reads=["pb%d" % bank, "recip"],
                    writes=["ot_b%d_%d" % (bank, side)], est=340)

        aexp = sb["rep" + s][:, 8:16].rearrange(
            "p (q o) -> p q o", o=1).to_broadcast([128, NPAIR, 65])
        S.op("dve", lambda e, a=aexp, St=St: e.tensor_tensor(
            sb[St][:, :].rearrange("p (q c) -> p q c", c=65),
            sb[St][:, :].rearrange("p (q c) -> p q c", c=65), a, OP.mult),
            reads=[St, "rep" + s, "Szb"], writes=[St], est=650)
        S.op("dve", lambda e, St=St: e.tensor_add(sb[St][:, :], sb[St][:, :],
                                                  sb["Pg_all"][:, :]),
             reads=[St, "Pg_2", "Pg_3"], writes=[St], est=650)
        return ["ot_b%d_%d" % (b, sd) for b, _, _ in NP_BANK for sd in range(2)]

    def wo_chunk(j, tc, ot_regs, sq_cols, sum_cols):
        banks = (4, 5) if j == 0 else (2, 3)
        S.op("sync", lambda e: e.dma_start(
            out=sb["ofm"].rearrange("p (c t) -> p c t", t=CK),
            in_=sb["otok"][:, :], transpose=True),
            reads=ot_regs, writes=["ofm_h0", "ofm"], dma=True, est=2500)

        def em_wo(e, c0, c1, bank_i):
            last = None
            for kc in range(KC):
                last = e.matmul(pb[bank_i][:, 0:c1 - c0],
                                sb["ofm"][:, kc * 128:(kc + 1) * 128],
                                sb["Wo"][:, kc * 1025 + c0:kc * 1025 + c1],
                                start=(kc == 0), stop=(kc == KC - 1))
            return last
        S.op("pe", lambda e: em_wo(e, 0, 512, banks[0]),
             reads=["ofm", "ofm_h0", "Wo", "Woh"], writes=["pb%d" % banks[0]],
             est=1800)
        S.op("pe", lambda e: em_wo(e, 512, 1024, banks[1]),
             reads=["ofm", "ofm_h0", "Wo", "Woh"], writes=["pb%d" % banks[1]],
             est=1800)

        dst = sb["o_f"] if j == 0 else sb["ffn_f"][:, 0:D]
        rg = ("of_h0", "of_h1") if j == 0 else ("ofb_h0", "ofb_h1")
        ex = ["qt_f"] if j == 0 else ["ffn0_h0", "ffn0_h1"]
        sm0, sm0_reg = sum_cols
        sq0, sq0_reg = sq_cols
        pbsrc = pbs[banks[0] // 2]
        S.op("act", lambda e, dst=dst, a=sm0, pbsrc=pbsrc: e.activation(
            dst[:, :], pbsrc[:, :], AF.Copy, accum_out=a),
            reads=["pb%d" % banks[0], "pb%d" % banks[1]],
            writes=[rg[0], rg[1], sm0_reg] + ex, est=1150)
        S.op("act", lambda e, a=sq0, dst=dst: e.activation(
            sb["sqk"][:, :], dst[:, :], AF.Square, accum_out=a),
            reads=[rg[0], rg[1]], writes=["sqk", sq0_reg], est=1150)
        return (dst, [rg[0], rg[1]])

    def tail_pair(m):
        a, b = 2 * m, 2 * m + 1
        sa, sb_ = str(a % 2), str(b % 2)
        xsa = sb["xs4"][:, (a % 4) * D:(a % 4) * D + D]
        xsb = sb["xs4"][:, (b % 4) * D:(b % 4) * D + D]
        xsa_reg = ["xs%d" % (a % 4), "xsh%d" % (a % 4)]
        xsb_reg = ["xs%d" % (b % 4), "xsh%d" % (b % 4)]
        suma_ap, suma_reg, m1a_ap, m1a_reg = ln1_infos.pop(a)
        sumb_ap, sumb_reg, m1b_ap, m1b_reg = ln1_infos.pop(b)

        ota = scan_chunk(a)
        sqo_ap, sqo_reg = stc("sqo_a")
        sqo2_ap, sqo2_reg = stc("sqo_b")
        assert ST["sqo_b"] == ST["sqo_a"] + 1
        so_ap, so_reg = stc("sumo_a")
        so2_ap, so2_reg = stc("sumo_b")
        assert ST["sumo_b"] == ST["sumo_a"] + 1
        srcs_a = wo_chunk(0, a, ota, (sqo_ap, sqo_reg), (so_ap, so_reg))
        otb = scan_chunk(b)
        srcs_b = wo_chunk(1, b, otb, (sqo2_ap, sqo2_reg), (so2_ap, so2_reg))

        sopair = sb["st"][:, ST["sumo_a"]:ST["sumo_a"] + 2]
        ssqpair = sb["st"][:, ST["sqo_a"]:ST["sqo_a"] + 2]
        rstd2p, bias2p, ln2_regs = ln_pair("ln2", sopair, ssqpair,
                                           [so_reg, so2_reg],
                                           [sqo_reg, sqo2_reg])
        for j, ((src_ap, src_regs), xs_, xreg, col) in enumerate(
                ((srcs_a, xsa, xsa_reg, 0), (srcs_b, xsb, xsb_reg, 1))):
            S.op("act", lambda e, i=src_ap, col=col,
                 r2=rstd2p, b2=bias2p: e.activation(
                sb["x2d"][:, col * D:(col + 1) * D],
                i[:, :], AF.Identity, bias=b2[:, col:col + 1],
                scale=r2[:, col:col + 1]),
                reads=src_regs + ln2_regs,
                writes=["x2_%d" % col], est=1150)
            S.op("dve", lambda e, xs_=xs_, col=col: e.tensor_add(
                sb["x2d"][:, col * D:(col + 1) * D],
                sb["x2d"][:, col * D:(col + 1) * D], xs_[:, :]),
                reads=xreg, writes=["x2_%d" % col], est=1180)
            r0 = row0(a + col)
            dma("sync", y_h[r0:r0 + CK, :], sb["x2d"][:, col * D:(col + 1) * D],
                reads=["x2_%d" % col], writes=["yh%d" % (a + col)])

        for dd in (a + 4, b + 4):
            if dd < ntc:
                load_x(dd % 4, row0(dd))

        s3a_ap, s3a_reg = stc("ssq3_a")
        s3b_ap, s3b_reg = stc("ssq3_b")
        assert ST["ssq3_b"] == ST["ssq3_a"] + 1
        s3pair = sb["st"][:, ST["ssq3_a"]:ST["ssq3_a"] + 2]
        S.op("act", lambda e, a_=s3a_ap: e.activation(
            sb["sqk"][:, :], sb["x2d"][:, 0:D], AF.Square, accum_out=a_),
            reads=["x2_0"], writes=["sqk", s3a_reg], est=1100)
        S.op("act", lambda e, a_=s3b_ap: e.activation(
            sb["sqk"][:, :], sb["x2d"][:, D:2 * D], AF.Square, accum_out=a_),
            reads=["x2_1"], writes=["sqk", s3b_reg], est=1100)
        assert ST[("sum1" + str(a % 4))] + 1 == ST[("ssq1" + str(a % 4))]
        su3a_ap, su3a_reg = stc("sum3_a")
        su3b_ap, su3b_reg = stc("sum3_b")
        assert ST["sum3_b"] == ST["sum3_a"] + 1
        su3pair = sb["st"][:, ST["sum3_a"]:ST["sum3_a"] + 2]
        S.op("dve", lambda e, o=su3a_ap, i=suma_ap: e.tensor_copy(o, i),
             reads=[suma_reg], writes=[su3a_reg])
        S.op("dve", lambda e, o=su3b_ap, i=sumb_ap: e.tensor_copy(o, i),
             reads=[sumb_reg], writes=[su3b_reg])
        rstd3p, bias3p, ln3_regs = ln_pair("ln3", su3pair, s3pair,
                                           [su3a_reg, su3b_reg],
                                           [s3a_reg, s3b_reg])

        r4_ap = [stc("r4_%d" % i) for i in range(4)]
        assert ST["r4_3"] == ST["r4_0"] + 3
        r4 = sb["st"][:, ST["r4_0"]:ST["r4_0"] + 4]
        r4_regs = [r for _, r in r4_ap]
        r4v = r4.rearrange("p (c t) -> p c t", t=2)
        x2v = sb["x2d"][:, :].rearrange("p (c d) -> p c d", c=2)
        S.op("dve", lambda e, o=r4v[:, :, 0:1], i=x2v: e.tensor_reduce(
            o, i, AX.X, OP.max),
            reads=["x2_0", "x2_1"], writes=[r4_regs[0], r4_regs[2]], est=2300)
        S.op("dve", lambda e, o=r4v[:, :, 1:2], i=x2v: e.tensor_reduce(
            o, i, AX.X, OP.min),
            reads=["x2_0", "x2_1"], writes=[r4_regs[1], r4_regs[3]], est=2300)
        rstd_bc = rstd3p.rearrange("p (c o) -> p c o", o=1).to_broadcast([128, 2, 2])
        bias_bc = bias3p.rearrange("p (c o) -> p c o", o=1).to_broadcast([128, 2, 2])
        S.op("dve", lambda e, v=r4v, sc=rstd_bc: e.tensor_tensor(v, v, sc, OP.mult),
             reads=r4_regs + ln3_regs, writes=r4_regs)
        S.op("dve", lambda e, v=r4v, bc=bias_bc: e.tensor_tensor(v, v, bc, OP.add),
             reads=r4_regs + ln3_regs, writes=r4_regs)
        ta_ap, ta_reg = stc("amax_a")
        tb_ap, tb_reg = stc("amax_b")
        assert ST["amax_b"] == ST["amax_a"] + 1
        amaxp = sb["st"][:, ST["amax_a"]:ST["amax_a"] + 2]
        S.op("dve", lambda e, o=amaxp, v=r4v: e.reduce_max(
            o, v, AX.X, apply_absolute_value=True),
            reads=r4_regs, writes=[ta_reg, tb_reg])
        lsp_ap = [stc("ls_a"), stc("ls_b")]
        assert ST["ls_b"] == ST["ls_a"] + 1
        lspair = sb["st"][:, ST["ls_a"]:ST["ls_a"] + 2]
        ls_regs = [lsp_ap[0][1], lsp_ap[1][1]]
        S.op("dve", lambda e, o=lspair, i=amaxp: e.tensor_scalar(
            o, i, 1e-12, 1.0 / 127.0, OP.max, OP.mult),
            reads=[ta_reg, tb_reg], writes=ls_regs)
        rlsp_ap = [stc("rls_a"), stc("rls_b")]
        assert ST["rls_b"] == ST["rls_a"] + 1
        rlspair = sb["st"][:, ST["rls_a"]:ST["rls_a"] + 2]
        rls_regs = [rlsp_ap[0][1], rlsp_ap[1][1]]
        S.op("dve", lambda e, o=rlspair, i=lspair: e.reciprocal(o, i),
             reads=ls_regs, writes=rls_regs)
        sqs_ap = [stc("sqs_a"), stc("sqs_b")]
        sqspair = sb["st"][:, ST["sqs_a"]:ST["sqs_a"] + 2]
        sqs_regs = [sqs_ap[0][1], sqs_ap[1][1]]
        S.op("dve", lambda e, o=sqspair, r3=rstd3p, rl=rlspair: e.tensor_tensor(
            o, r3, rl, OP.mult),
            reads=ln3_regs + rls_regs, writes=sqs_regs)
        sqb_ap = [stc("sqb_a"), stc("sqb_b")]
        sqbpair = sb["st"][:, ST["sqb_a"]:ST["sqb_a"] + 2]
        sqb_regs = [sqb_ap[0][1], sqb_ap[1][1]]
        S.op("dve", lambda e, o=sqbpair, b3=bias3p, rl=rlspair: e.tensor_tensor(
            o, b3, rl, OP.mult),
            reads=ln3_regs + rls_regs, writes=sqb_regs)
        S.op("dve", lambda e, o=sqbpair: e.tensor_scalar_add(o, o, 0.5),
             reads=sqb_regs, writes=sqb_regs)

        facc_banks = []
        sfh = [stc("sfh_%d" % i) for i in range(2)]
        assert ST["sfh_1"] == ST["sfh_0"] + 1
        for col in range(2):
            x2c = sb["x2d"][:, col * D:(col + 1) * D]
            qtb = sb["wrep"]
            S.op("dve", lambda e, col=col, x2c=x2c, qtb=qtb: e.tensor_scalar(
                qtb[:, :], x2c,
                sqspair[:, col:col + 1], sqbpair[:, col:col + 1],
                OP.mult, OP.add),
                reads=["x2_%d" % col] + sqs_regs + sqb_regs,
                writes=["qtb", "wrep", "wrep_h0"],
                est=650)
            S.op("act", lambda e, qtb=qtb: e.activation(sb["qi"][:, :],
                                                        qtb[:, :], AF.Copy),
                 reads=["qtb"], writes=["qi"], est=1150)
            gqn = "gq" if col == 0 else "gq2"
            S.op("dve", lambda e, qtb=qtb, gqn=gqn: e.tensor_tensor(
                sb[gqn][:, :], sb["qi"][:, :], qtb[:, :], OP.is_gt),
                reads=["qi", "qtb"], writes=[gqn], est=1180)
            qln = "ql" if col == 0 else "ql2"
            S.op("dve", lambda e, qln=qln, gqn=gqn: e.scalar_tensor_tensor(
                sb[qln][:, :], sb[gqn][:, :], -1.0, sb["qi"][:, :],
                OP.mult, OP.add),
                reads=[gqn, "qi"], writes=[qln], est=1225)

            qb = (4, 5)
            qfn = "qlf" if col == 0 else "qlf2"
            S.op("sync", lambda e, qln=qln, qfn=qfn: e.dma_start(
                out=sb[qfn].rearrange("p (c t) -> p c t", t=CK),
                in_=sb[qln][:, :], transpose=True),
                reads=[qln], writes=[qfn + "_h0", qfn], dma=True, est=2500)

            def em_ffn(e, c0, c1, bank_i, qfn):
                last = None
                for kc in range(KC):
                    last = e.matmul(pb[bank_i][:, 0:c1 - c0],
                                    sb[qfn][:, kc * 128:(kc + 1) * 128],
                                    sb["QR"][:, kc * 1025 + c0:kc * 1025 + c1],
                                    start=(kc == 0), stop=(kc == KC - 1))
                return last
            qr_regs = ["QR_c%d" % c for c in range(KC)]
            S.op("pe", lambda e, qb=qb, qfn=qfn: em_ffn(e, 0, 512, qb[0], qfn),
                 reads=[qfn, qfn + "_h0"] + qr_regs, writes=["pb%d" % qb[0]],
                 est=1800)
            S.op("pe", lambda e, qb=qb, qfn=qfn: em_ffn(e, 512, 1024, qb[1], qfn),
                 reads=[qfn, qfn + "_h0"] + qr_regs, writes=["pb%d" % qb[1]],
                 est=1800)

            ex = ["ofb_h0", "ofb_h1"] if col == 0 else []
            sfh0, sfh0_reg = sfh[col]
            S.op("act", lambda e, col=col, a=sfh0: e.activation(
                sb["ffn_f"][:, col * D:(col + 1) * D], pbs[2][:, :], AF.Copy,
                accum_out=a),
                reads=["pb4", "pb5"],
                writes=["ffn%d_h0" % col, "ffn%d_h1" % col, sfh0_reg] + ex,
                est=1150)
            facc_banks.append(
                (sb["ffn_f"][:, col * D:(col + 1) * D],
                 ["ffn%d_h0" % col, "ffn%d_h1" % col]))

        sfpair = sb["st"][:, ST["sfh_0"]:ST["sfh_0"] + 2]
        sf_regs = [sfh[0][1], sfh[1][1]]
        S.op("dve", lambda e, o=sfpair, lsP=lspair: e.tensor_tensor(
            o, o, lsP, OP.mult),
            reads=sf_regs + ls_regs, writes=sf_regs)
        sqf_ap = [stc("ssqf_a"), stc("ssqf_b")]
        sqfpair = sb["st"][:, ST["ssqf_a"]:ST["ssqf_a"] + 2]
        sqf_regs = [sqf_ap[0][1], sqf_ap[1][1]]
        assert ST["ssqf_b"] == ST["ssqf_a"] + 1
        for col in range(2):
            src_ap, src_regs = facc_banks[col]
            S.op("act", lambda e, a_=sqf_ap[col][0], i=src_ap: e.activation(
                sb["sqk"][:, :], i[:, :], AF.Square, accum_out=a_),
                reads=src_regs, writes=["sqk", sqf_regs[col]], est=1150)
        S.op("dve", lambda e, o=sqfpair, lsP=lspair: e.tensor_tensor(
            o, o, lsP, OP.mult), reads=sqf_regs + ls_regs, writes=sqf_regs)
        S.op("dve", lambda e, o=sqfpair, lsP=lspair: e.tensor_tensor(
            o, o, lsP, OP.mult), reads=sqf_regs + ls_regs, writes=sqf_regs)
        rstd4p, bias4p, ln4_regs = ln_pair("ln4", sfpair, sqfpair, sf_regs,
                                           sqf_regs)
        sc4_ap = [stc("sc4_a"), stc("sc4_b")]
        sc4pair = sb["st"][:, ST["sc4_a"]:ST["sc4_a"] + 2]
        sc4_regs = [sc4_ap[0][1], sc4_ap[1][1]]
        S.op("dve", lambda e, o=sc4pair, lsP=lspair, r4P=rstd4p: e.tensor_tensor(
            o, lsP, r4P, OP.mult),
            reads=ls_regs + ln4_regs, writes=sc4_regs)
        for col in range(2):
            fbuf = sb["fn"] if col == 0 else sb["o_n"]
            fregs = ("fn_h0", "fn") if col == 0 else ("fnB_h0", "fnB")
            fex = [] if col == 0 else ["o_n"]
            src_ap, src_regs = facc_banks[col]
            S.op("act", lambda e, i=src_ap, col=col, fbuf=fbuf,
                 b4=bias4p, s4=sc4pair: e.activation(
                fbuf[:, :], i[:, :],
                AF.Identity, bias=b4[:, col:col + 1],
                scale=s4[:, col:col + 1]),
                reads=src_regs + ln4_regs + sc4_regs,
                writes=[fregs[0], fregs[1]] + fex, est=1150)
            r0 = row0(a + col)
            S.op("pool", lambda e, r0=r0, fbuf=fbuf: e.dma_start(
                out=y_h[r0:r0 + CK, :], in_=fbuf[:, :], accum_op=OP.add),
                reads=[fregs[0], fregs[1], "yh%d" % (a + col)],
                writes=["yh%d" % (a + col)], dma=True, est=2500)

    def ln_pair(tag, sum2, ssq2, sum_regs, ssq_regs, n=D):
        m2_ap = [stc(tag + "_m_a"), stc(tag + "_m_b")]
        m2 = sb["st"][:, ST[tag + "_m_a"]:ST[tag + "_m_a"] + 2]
        m_regs = [m2_ap[0][1], m2_ap[1][1]]
        nv_ap = [stc(tag + "_nv_a"), stc(tag + "_nv_b")]
        nv2 = sb["st"][:, ST[tag + "_nv_a"]:ST[tag + "_nv_a"] + 2]
        nv_regs = [nv_ap[0][1], nv_ap[1][1]]
        r_ap = [stc(tag + "_r_a"), stc(tag + "_r_b")]
        r2 = sb["st"][:, ST[tag + "_r_a"]:ST[tag + "_r_a"] + 2]
        r_regs = [r_ap[0][1], r_ap[1][1]]
        b_ap = [stc(tag + "_b_a"), stc(tag + "_b_b")]
        b2 = sb["st"][:, ST[tag + "_b_a"]:ST[tag + "_b_a"] + 2]
        b_regs = [b_ap[0][1], b_ap[1][1]]
        ex_ap = [stc(tag + "_e_a"), stc(tag + "_e_b")]
        ex2 = sb["st"][:, ST[tag + "_e_a"]:ST[tag + "_e_a"] + 2]
        ex_regs = [ex_ap[0][1], ex_ap[1][1]]
        S.op("dve", lambda e: e.tensor_scalar_mul(m2, sum2, 1.0 / n),
             reads=sum_regs, writes=m_regs)
        S.op("dve", lambda e: e.tensor_scalar_mul(ex2, ssq2, 1.0 / n),
             reads=ssq_regs, writes=ex_regs)
        S.op("dve", lambda e: e.tensor_tensor(nv2, m2, m2, OP.mult),
             reads=m_regs, writes=nv_regs)
        S.op("dve", lambda e: e.tensor_tensor(nv2, nv2, ex2, OP.subtract),
             reads=nv_regs + ex_regs, writes=nv_regs)
        S.op("act", lambda e: e.activation(nv2, nv2, AF.Ln, bias=c_eps,
                                           scale=c_neg1),
             reads=nv_regs + ["cst0", "cst1"], writes=nv_regs)
        S.op("act", lambda e: e.activation(r2, nv2, AF.Exp, scale=c_negh),
             reads=nv_regs + ["cst2"], writes=r_regs)
        S.op("dve", lambda e: e.scalar_tensor_tensor(
            b2, m2, -1.0, r2, OP.mult, OP.mult),
            reads=m_regs + r_regs, writes=b_regs)
        return r2, b2, r_regs + b_regs

    dbg_h = {}
    if debug_tc is not None:
        for name, _regs in DBG_TENSORS:
            ap = sb[name]
            dbg_h[name] = nc.dram_tensor("dbg_" + name, list(ap.shape), ap.dtype,
                                         kind="ExternalOutput").ap()

    def run_all(g):
        for _ in g:
            pass

    assert ntc % 2 == 0 and spc % 2 == 0
    run_all(front(0))
    run_all(front(1))
    for m in range(ntc // 2):
        tail_pair(m)
        if 2 * m + 2 < ntc:
            run_all(front(2 * m + 2))
        if 2 * m + 3 < ntc:
            run_all(front(2 * m + 3))

    return nc, S


def make_consts():
    idb = np.eye(128, dtype=np.float32).astype(ml_dtypes.bfloat16)
    idf = np.eye(128, dtype=np.float32)
    i16 = np.eye(H, dtype=np.float32)
    m = (np.arange(128)[None, :] >= np.arange(128)[:, None]).astype(np.float32)
    mask4 = np.concatenate([m, m, m, m], axis=1).astype(ml_dtypes.bfloat16)
    selhi = np.zeros((1, 128), dtype=np.float32)
    selhi[0, 0:64] = 1.0
    sello = np.zeros((1, 128), dtype=np.float32)
    sello[0, 64:128] = 1.0
    ones1 = np.ones((1, 128), dtype=np.float32)
    return {"idb": idb, "idf": idf, "i16": i16, "mask4": mask4,
            "selhi": selhi, "sello": sello, "ones1": ones1}


def build(nseq, spc, trunc=None, debug_tc=None):
    nc, S = build_program(nseq, spc, debug_tc=debug_tc)
    final = {"sync": [("qs%d" % i, None) for i in range(NQ)],
             "pool": [("qg%d" % i, None) for i in range(NQ)]}
    return _emit_scheduled(nc, S, extra_final=final, trunc=trunc)


def kernel(L0_attn_x, L0_attn_Wq, L0_attn_Wk, L0_attn_Wv, L0_attn_Wo,
           L0_attn_Wa, L0_attn_ba, L0_ffn_rhs):
    x = np.ascontiguousarray(np.asarray(L0_attn_x, dtype=np.float32))
    B, N, _ = x.shape
    nseq, spc = 2, N // CK
    nc = build(nseq, spc)
    common = {
        "wq": np.ascontiguousarray(L0_attn_Wq, dtype=np.float32),
        "wk": np.ascontiguousarray(L0_attn_Wk, dtype=np.float32),
        "wv": np.ascontiguousarray(L0_attn_Wv, dtype=np.float32),
        "wo": np.ascontiguousarray(L0_attn_Wo, dtype=np.float32),
        "wa": np.ascontiguousarray(L0_attn_Wa, dtype=np.float32),
        "ba": np.ascontiguousarray(L0_attn_ba, dtype=np.float32),
        "rhs": np.ascontiguousarray(L0_ffn_rhs, dtype=np.float32),
    }
    common.update(make_consts())
    in_maps = []
    for c in range(8):
        m = dict(common)
        m["x"] = np.ascontiguousarray(x[2 * c:2 * c + 2].reshape(nseq * N, D))
        in_maps.append(m)
    res = run_bass_kernel_spmd(nc, in_maps, core_ids=list(range(8)))
    out = np.empty((B, N, D), dtype=np.float32)
    for c in range(8):
        out[2 * c:2 * c + 2] = res.results[c]["y"].reshape(2, N, D)
    return out


# version 91
# speedup vs baseline: 1.0163x; 1.0163x over previous
import numpy as np
import ml_dtypes
import concourse.bass as bass
import concourse.mybir as mybir
from concourse.bass_utils import run_bass_kernel_spmd

F32 = mybir.dt.float32
BF16 = mybir.dt.bfloat16
I32 = mybir.dt.int32
I8 = mybir.dt.int8
OP = mybir.AluOpType
AF = mybir.ActivationFunctionType
AX = mybir.AxisListType

D = 1024
H = 16
CK = 128
KC = D // 128
NPAIR = H // 2
LN_EPS = 1e-5
A_EPS = 1e-6
A_SHIFT = 2.0

ENGINES = ("pe", "dve", "act", "pool", "sync")
NQ = 4
SEM_NAMES = ("pe", "dve", "act", "pool") + tuple(
    "%s%d" % (q, i) for q in ("qs", "qg") for i in range(NQ))


class Sched:

    def __init__(self):
        self.nodes = []
        self.reg = {}

    def op(self, eng, emit, reads=(), writes=(), dma=False, est=400):
        pbr = [r for r in reads if r.startswith("pb")]
        reads = [r for r in reads if not r.startswith("pb")]
        writes = list(writes) + pbr
        nid = len(self.nodes)
        preds = set()
        for r in reads:
            st = self.reg.get(r)
            if st and st["w"] is not None:
                preds.add(st["w"])
        for w in writes:
            st = self.reg.get(w)
            if st:
                if st["w"] is not None:
                    preds.add(st["w"])
                preds.update(st["r"])
        preds.discard(nid)
        self.nodes.append({"eng": eng, "emit": emit, "dma": dma, "est": est,
                           "preds": preds, "id": nid})
        for r in reads:
            st = self.reg.setdefault(r, {"w": None, "r": []})
            st["r"].append(nid)
        for w in writes:
            self.reg[w] = {"w": nid, "r": []}
        return nid

    def schedule(self):
        import heapq
        LAT = 320.5
        DMA_ISSUE = 1600.0
        n = len(self.nodes)
        succs = [[] for _ in range(n)]
        npred = [0] * n
        for nd in self.nodes:
            npred[nd["id"]] = len(nd["preds"])
            for p in nd["preds"]:
                succs[p].append(nd["id"])
        cp = [0] * n
        for i in range(n - 1, -1, -1):
            nd = self.nodes[i]
            best = 0
            for sc in succs[i]:
                v = cp[sc] + (LAT if self.nodes[sc]["eng"] != nd["eng"] else 0)
                if v > best:
                    best = v
            cp[i] = best + nd["est"]
        eng_clock = {e: 0.0 for e in ENGINES}
        finish = [0.0] * n
        future = {e: [] for e in ENGINES}
        runnable = {e: [] for e in ENGINES}
        for i, nd in enumerate(self.nodes):
            if npred[i] == 0:
                heapq.heappush(future[nd["eng"]], (0.0, -cp[i], i))
        order = {e: [] for e in ENGINES}
        gseq = []
        self.gseq = gseq
        scheduled = 0

        def promote(e, upto):
            f = future[e]
            while f and f[0][0] <= upto:
                es_, ncp_, i_ = heapq.heappop(f)
                heapq.heappush(runnable[e], (ncp_, es_, i_))

        while scheduled < n:
            bs, bcp, be = None, None, None
            for e in ENGINES:
                promote(e, eng_clock[e])
                if runnable[e]:
                    s = eng_clock[e]
                    c = -runnable[e][0][0]
                elif future[e]:
                    s = max(future[e][0][0], eng_clock[e])
                    c = -future[e][0][1]
                else:
                    continue
                if bs is None or s < bs - 1e-9 or (s < bs + 1e-9 and c > bcp):
                    bs, bcp, be = s, c, e
            e = be
            promote(e, bs)
            if runnable[e]:
                ncp_, es_, i = heapq.heappop(runnable[e])
            else:
                es_, ncp_, i = heapq.heappop(future[e])
            nd = self.nodes[i]
            start = max(bs, es_)
            busy = min(nd["est"], DMA_ISSUE) if nd["dma"] else nd["est"]
            eng_clock[e] = start + busy
            finish[i] = start + nd["est"]
            order[e].append(i)
            gseq.append(i)
            scheduled += 1
            for sc in succs[i]:
                npred[sc] -= 1
                if npred[sc] == 0:
                    nsc = self.nodes[sc]
                    es = 0.0
                    for p in nsc["preds"]:
                        v = finish[p] + (LAT if self.nodes[p]["eng"] != nsc["eng"]
                                         else 0.0)
                        if v > es:
                            es = v
                    heapq.heappush(future[nsc["eng"]], (es, -cp[sc], sc))
        return order


def _emit_scheduled(nc, S, extra_final=(), trunc=None):
    import contextlib
    order = S.schedule()
    nodes = S.nodes
    if trunc is not None:
        pos = {}
        k = 0
        flat = []
        iters = {e: iter(order[e]) for e in ENGINES}
        gseq = S.gseq
        keep = set(gseq[:trunc])
        order = {e: [i for i in order[e] if i in keep] for e in ENGINES}
    cnt = {s: 0 for s in SEM_NAMES}
    qn = {"qs": 0, "qg": 0}
    info = {}
    for eng in ENGINES:
        for i in order[eng]:
            nd = nodes[i]
            if nd["dma"]:
                q = {"sync": "qs", "pool": "qg"}[eng]
                k = qn[q]
                qn[q] += 1
                sem = "%s%d" % (q, k % NQ)
                step = 16
            else:
                sem = eng
                step = 1
            cnt[sem] += step
            info[i] = (sem, cnt[sem], step)
    stack = contextlib.ExitStack()
    sems = {s: stack.enter_context(nc.semaphore("sem_" + s)) for s in SEM_NAMES}
    blk = stack.enter_context(nc.Block())

    def runner(ename):
        def run(eng):
            seen = {}
            for i in order[ename]:
                nd = nodes[i]
                sem, mycnt, step = info[i]
                waits = {}
                if nd["dma"] and mycnt > 16:
                    waits[sem] = mycnt - 16
                for p in nd["preds"]:
                    psem, pcnt, _ = info[p]
                    if psem == "pe" and ename == "pe":
                        continue
                    if waits.get(psem, 0) < pcnt:
                        waits[psem] = pcnt
                for s_, c_ in waits.items():
                    if c_ > seen.get(s_, 0):
                        seen[s_] = c_
                        eng.wait_ge(sems[s_], c_)
                nd["emit"](eng).then_inc(sems[sem], step)
            for s_, c_ in extra_final.get(ename, []):
                eng.wait_ge(sems[s_], cnt[s_] if c_ is None else c_)
        return run

    blk.sync(runner("sync"))
    blk.tensor(runner("pe"))
    blk.vector(runner("dve"))
    blk.scalar(runner("act"))
    blk.gpsimd(runner("pool"))
    stack.close()
    return nc


DBG_TENSORS = []


def build_program(nseq, spc, debug_tc=None):
    ntc = nseq * spc
    NT = ntc * CK
    nc = bass.Bass("TRN2", target_bir_lowering=False)

    x_h = nc.dram_tensor("x", [NT, D], F32, kind="ExternalInput").ap()
    y_h = nc.dram_tensor("y", [NT, D], F32, kind="ExternalOutput").ap()
    wq_h = nc.dram_tensor("wq", [D, D], F32, kind="ExternalInput").ap()
    wk_h = nc.dram_tensor("wk", [D, D], F32, kind="ExternalInput").ap()
    wv_h = nc.dram_tensor("wv", [D, D], F32, kind="ExternalInput").ap()
    wo_h = nc.dram_tensor("wo", [D, D], F32, kind="ExternalInput").ap()
    wa_h = nc.dram_tensor("wa", [D, H], F32, kind="ExternalInput").ap()
    ba_h = nc.dram_tensor("ba", [H], F32, kind="ExternalInput").ap()
    rhs_h = nc.dram_tensor("rhs", [D, D], F32, kind="ExternalInput").ap()
    idb_h = nc.dram_tensor("idb", [128, 128], BF16, kind="ExternalInput").ap()
    i16_h = nc.dram_tensor("i16", [H, H], F32, kind="ExternalInput").ap()
    idf_h = nc.dram_tensor("idf", [128, 128], F32, kind="ExternalInput").ap()
    mask_h = nc.dram_tensor("mask4", [128, 512], BF16, kind="ExternalInput").ap()
    selhi_h = nc.dram_tensor("selhi", [1, 128], F32, kind="ExternalInput").ap()
    sello_h = nc.dram_tensor("sello", [1, 128], F32, kind="ExternalInput").ap()
    ones1_h = nc.dram_tensor("ones1", [1, 128], F32, kind="ExternalInput").ap()

    sb = {}

    def T(name, shape, dt):
        sb[name] = nc.alloc_sbuf_tensor("s_" + name, list(shape), dt).ap()
        return sb[name]

    T("Wq", (128, KC * 1024), BF16)
    T("Wk", (128, KC * 1024), BF16)
    T("Wv", (128, KC * 1024), BF16)
    T("Wa", (128, KC * H), BF16)
    T("Wo", (128, KC * 1025), BF16)
    T("QR", (128, KC * 1025), BF16)
    T("idb", (128, 128), BF16)
    T("i16", (H, H), F32)
    T("idf", (128, 128), F32)
    T("mask4", (128, 512), BF16)
    T("selhi", (1, 128), F32)
    T("sello", (1, 128), F32)
    T("ones1", (1, 128), F32)
    T("zeros16", (H, CK), F32)
    T("xs4", (128, 4 * D), F32)
    T("x2d", (128, 2 * D), F32)
    for pf in range(2):
        s = str(pf)
        T("xn" + s, (128, D), BF16)
        T("xnf" + s, (128, D), BF16)
        T("qpre" + s, (128, D), BF16)
        T("ktok" + s, (128, D), BF16)
        T("vz" + s, (128, H * 65), BF16)
        T("qAB" + s, (128, 2 * D), BF16)
        T("kfm" + s, (128, D), BF16)
        T("kapT" + s, (128, H), F32)
        T("rep" + s, (128, 24), F32)
        T("lu" + s, (H, CK), F32)
        T("le" + s, (H, CK), F32)
        T("ll" + s, (H, CK), F32)
        T("lcum" + s, (H, CK), F32)
        T("karg" + s, (H, CK), F32)
        T("ext" + s, (H, 4), F32)
        T("prow" + s, (1, 48), F32)
    T("scm", (128, H * 128), BF16)
    T("sqj", (128, D), BF16)
    T("sqk", (128, D), BF16)
    T("e_sb", (128, D), BF16)
    T("r_sb", (128, D), BF16)
    T("otok", (128, D), BF16)
    T("ofm", (128, D), BF16)
    T("o_n", (128, D), F32)
    T("o_f", (128, D), F32)
    T("ffn_f", (128, 2 * D), F32)
    T("S0", (128, NPAIR * 65), F32)
    T("S1", (128, NPAIR * 65), F32)
    T("Szb", (128, NPAIR * 65), BF16)
    T("Pg_all", (128, NPAIR * 65), F32)
    T("qi", (128, D), I8)
    T("gq", (128, D), BF16)
    T("ql", (128, D), BF16)
    T("ql2", (128, D), BF16)
    T("gq2", (128, D), BF16)
    T("qlf2", (128, D), BF16)
    T("qlf", (128, D), BF16)
    T("fn", (128, D), F32)
    T("wrep", (128, D), F32)
    T("den_all", (128, H), F32)
    T("denf", (128, H), F32)
    T("recip", (128, H), F32)
    T("rsT8", (128, KC), F32)
    T("ecol", (128, 1), F32)
    T("ba2", (H, 1), F32)
    T("st", (128, 120), F32)
    T("cst", (128, 4), F32)
    c_eps = sb["cst"][:, 0:1]
    c_neg1 = sb["cst"][:, 1:2]
    c_negh = sb["cst"][:, 2:3]
    c_one = sb["cst"][:, 3:4]

    ST = {}
    _stn = [0]

    def stc(name):
        if name not in ST:
            ST[name] = _stn[0]
            _stn[0] += 1
            assert _stn[0] <= 120
        i = ST[name]
        return sb["st"][:, i:i + 1], "st_" + name

    pbs = [nc.alloc_psum_tensor("pbs%d" % i, [128, 1024], F32).ap()
           for i in range(3)]
    pb = [pbs[i // 2][:, (i % 2) * 512:(i % 2 + 1) * 512] for i in range(6)]
    pb.append(nc.alloc_psum_tensor("pb6", [128, 512], F32).ap())
    pb.append(nc.alloc_psum_tensor("pb7", [128, 512], F32).ap())

    S = Sched()

    def dma(q, out, in_, reads=(), writes=()):
        S.op(q, lambda e, o=out, i=in_: e.dma_start(out=o, in_=i),
             reads=reads, writes=writes, dma=True, est=2500)

    assert nseq == 2

    def row0(tc):
        return (tc % 2) * spc * CK + (tc // 2) * CK

    dma("sync", sb["idb"][:, :], idb_h, writes=["idb"])
    dma("sync", sb["i16"][:, :], i16_h, writes=["i16"])
    dma("sync", sb["idf"][:, :], idf_h, writes=["idf"])
    dma("sync", sb["mask4"][:, :], mask_h, writes=["mask4"])
    dma("sync", sb["selhi"][:, :], selhi_h, writes=["selhi"])
    dma("sync", sb["sello"][:, :], sello_h, writes=["sello"])
    dma("sync", sb["ones1"][:, :], ones1_h, writes=["ones1"])
    dma("sync", sb["ba2"][:, :], ba_h.rearrange("(p o) -> p o", o=1), writes=["ba2"])
    def load_x(slot, r0):
        dma("sync", sb["xs4"][:, slot * D:slot * D + 512],
            x_h[r0:r0 + CK, 0:512], writes=["xs%d" % slot])
        dma("pool", sb["xs4"][:, slot * D + 512:(slot + 1) * D],
            x_h[r0:r0 + CK, 512:1024], writes=["xsh%d" % slot])

    for i0 in range(min(4, ntc)):
        load_x(i0, row0(i0))
    for nm, hb in (("Wq", wq_h), ("Wk", wk_h), ("Wv", wv_h)):
        tgt = sb[nm].rearrange("p (c d) -> p c d", c=KC)
        srcr = hb.rearrange("(c p) d -> p c d", p=128)
        dma("pool", tgt[:, 0:KC // 2, :], srcr[:, 0:KC // 2, :], writes=[nm])
        dma("pool", tgt[:, KC // 2:KC, :], srcr[:, KC // 2:KC, :],
            writes=[nm + "h"])
    dma("pool", sb["Wa"].rearrange("p (c d) -> p c d", c=KC),
        wa_h.rearrange("(c p) d -> p c d", p=128), writes=["Wa"])
    tgt = sb["Wo"].rearrange("p (c d) -> p c d", c=KC, d=1025)[:, :, 0:1024]
    srcr = wo_h.rearrange("(c p) d -> p c d", p=128)
    dma("pool", tgt[:, 0:KC // 2, :], srcr[:, 0:KC // 2, :], writes=["Wo"])
    dma("pool", tgt[:, KC // 2:KC, :], srcr[:, KC // 2:KC, :], writes=["Woh"])

    for pf in range(2):
        s = str(pf)
        S.op("pool", lambda e, s=s: e.memset(
            sb["vz" + s].rearrange("p (h c) -> p h c", c=65)[:, :, 64:65], 1.0),
            writes=["vz_ones%s" % s])
    S.op("pool", lambda e: e.memset(sb["zeros16"][:, :], 0.0), writes=["zeros16"])
    for pf in range(2):
        s = str(pf)
        S.op("pool", lambda e, s=s: e.memset(sb["qAB" + s][:, :], 0.0),
             writes=["qAB%s_pad" % s])
    for i, v in ((0, LN_EPS), (1, -1.0), (2, -0.5), (3, 1.0)):
        S.op("pool", lambda e, i=i, v=v: e.memset(sb["cst"][:, i:i + 1], v),
             writes=["cst%d" % i])
    S.op("dve", lambda e: e.tensor_scalar_add(sb["ba2"][:, :], sb["ba2"][:, :], A_SHIFT),
         reads=["ba2"], writes=["ba2"])
    for ld in range(KC):
        buf = "o_n" if ld % 2 == 0 else "fn"
        dma("sync" if ld % 2 == 0 else "pool", sb[buf][:, :],
            rhs_h.rearrange("(c p) d -> p c d", p=128)[:, ld, :],
            writes=[buf])
        if ld == 0:
            S.op("dve", lambda e, buf=buf: e.tensor_copy(
                sb["o_f"][:, :], sb[buf][:, :]),
                reads=[buf], writes=["qt_f"])
            S.op("dve", lambda e, buf=buf: e.tensor_copy(
                sb["x2d"][:, 1024:2048], sb[buf][:, :]),
                reads=[buf], writes=["x2_1"])
        else:
            S.op("dve", lambda e, buf=buf: e.tensor_tensor(
                sb["o_f"][:, :], sb["o_f"][:, :], sb[buf][:, :], OP.max),
                reads=["qt_f", buf], writes=["qt_f"])
            S.op("dve", lambda e, buf=buf: e.tensor_tensor(
                sb["x2d"][:, 1024:2048], sb["x2d"][:, 1024:2048], sb[buf][:, :],
                OP.min),
                reads=["x2_1", buf], writes=["x2_1"])
    S.op("dve", lambda e: e.tensor_scalar_mul(sb["x2d"][:, 1024:2048], sb["x2d"][:, 1024:2048], -1.0),
         reads=["x2_1"], writes=["x2_1"])
    S.op("dve", lambda e: e.tensor_tensor(
        sb["o_f"][:, :], sb["o_f"][:, :], sb["x2d"][:, 1024:2048], OP.max),
        reads=["qt_f", "x2_1"], writes=["qt_f"])
    S.op("dve", lambda e: e.tensor_copy(sb["sqj"][:, :], sb["o_f"][:, :]),
         reads=["qt_f"], writes=["sqj"])
    for c in range(KC):
        S.op("pe", lambda e, c=c: e.matmul(
            pb[c // 4][:, (c % 4) * 128:(c % 4) * 128 + 128],
            sb["sqj"][:, c * 128:(c + 1) * 128], sb["idb"][:, :],
            start=True, stop=True),
            reads=["sqj", "idb"], writes=["pb%d" % (c // 4)])
    for c in range(KC):
        S.op("dve", lambda e, c=c: e.reduce_max(
            sb["rsT8"][:, c:c + 1], pb[c // 4][:, (c % 4) * 128:(c % 4) * 128 + 128],
            AX.X), reads=["pb%d" % (c // 4)], writes=["rsT8"])
    S.op("dve", lambda e: e.tensor_scalar(
        sb["rsT8"][:, :], sb["rsT8"][:, :], 1e-12, 1.0 / 127.0, OP.max, OP.mult),
        reads=["rsT8"], writes=["rsT8"])
    for half in range(2):
        def em_rr(e, half=half):
            last = None
            for j in range(4):
                c = half * 4 + j
                last = e.matmul(pb[2 + half][0:1, j * 128:(j + 1) * 128],
                                sb["rsT8"][:, c:c + 1], sb["idf"][:, :],
                                start=True, stop=True)
            return last
        S.op("pe", em_rr, reads=["rsT8", "idf"], writes=["pb%d" % (2 + half)])
    for half in range(2):
        S.op("dve", lambda e, half=half: e.tensor_copy(
            sb["x2d"][0:1, 0:1024][:, half * 512:(half + 1) * 512], pb[2 + half][0:1, :]),
            reads=["pb%d" % (2 + half)], writes=["x2_0" if half else "x2_1"])
    for half in range(2):
        S.op("pe", lambda e, half=half: e.matmul(
            pb[4 + half][:, :], sb["ones1"][:, :],
            sb["x2d"][0:1, 0:1024][:, half * 512:(half + 1) * 512], start=True, stop=True),
            reads=["x2_0", "x2_1", "ones1"], writes=["pb%d" % (4 + half)])
    for half in range(2):
        S.op("dve", lambda e, half=half: e.tensor_copy(
            sb["wrep"][:, half * 512:(half + 1) * 512], pb[4 + half][:, :]),
            reads=["pb%d" % (4 + half)], writes=["wrep" if half else "wrep_h0"])
    S.op("dve", lambda e: e.reciprocal(sb["fn"][:, :], sb["wrep"][:, :]),
         reads=["wrep", "wrep_h0"], writes=["fn"])

    for c in range(KC):
        buf, breg = (("o_n", "o_n") if c % 2 == 0
                     else ("x2d", "x2_1"))
        bap = sb["o_n"][:, :] if c % 2 == 0 else sb["x2d"][:, 1024:2048]
        dma("sync" if c % 2 == 0 else "pool", bap,
            rhs_h.rearrange("(c p) d -> p c d", p=128)[:, c, :],
            writes=[breg])
        S.op("dve", lambda e, bap=bap: e.tensor_tensor(
            sb["o_f"][:, :], bap, sb["fn"][:, :], OP.mult),
            reads=[breg, "fn"], writes=["qt_f"])
        S.op("dve", lambda e: e.tensor_scalar_add(
            sb["o_f"][:, :], sb["o_f"][:, :], 0.5),
            reads=["qt_f"], writes=["qt_f"])
        S.op("act", lambda e: e.activation(sb["qi"][:, :], sb["o_f"][:, :], AF.Copy),
             reads=["qt_f"], writes=["qi"])
        S.op("dve", lambda e: e.tensor_tensor(
            sb["gq"][:, :], sb["qi"][:, :], sb["o_f"][:, :], OP.is_gt),
            reads=["qi", "qt_f"], writes=["gq"], est=1180)
        S.op("dve", lambda e, c=c: e.scalar_tensor_tensor(
            sb["QR"][:, c * 1025:c * 1025 + 1024], sb["gq"][:, :], -1.0,
            sb["qi"][:, :], OP.mult, OP.add),
            reads=["gq", "qi"], writes=["QR_c%d" % c])
    for c in range(KC):
        S.op("dve", lambda e, c=c: e.tensor_tensor(
            sb["QR"][:, c * 1025:c * 1025 + 1024],
            sb["QR"][:, c * 1025:c * 1025 + 1024],
            sb["wrep"][:, :], OP.mult),
            reads=["QR_c%d" % c, "wrep", "wrep_h0"], writes=["QR_c%d" % c],
            est=1180)

    def ln_scalars(tag, sum_reg, sumsq_reg, sum_ap, sumsq_ap, n=D, adj=None):
        m_ap, m_reg = stc(tag + "_m")
        exq_ap, exq_reg = stc(tag + "_exq")
        nv_ap, nv_reg = stc(tag + "_nv")
        rstd_ap, rstd_reg = stc(tag + "_rstd")
        lnv_ap, lnv_reg = stc(tag + "_lnv")
        bias_ap, bias_reg = stc(tag + "_bias")
        if adj is not None and ST[tag + "_exq"] == ST[tag + "_m"] + 1:
            m2 = sb["st"][:, ST[tag + "_m"]:ST[tag + "_m"] + 2]
            S.op("dve", lambda e, m2=m2, adj=adj: e.tensor_scalar_mul(
                m2, adj, 1.0 / n),
                reads=[sum_reg, sumsq_reg], writes=[m_reg, exq_reg])
        else:
            S.op("dve", lambda e: e.tensor_scalar_mul(m_ap, sum_ap, 1.0 / n),
                 reads=[sum_reg], writes=[m_reg])
            S.op("dve", lambda e: e.tensor_scalar_mul(exq_ap, sumsq_ap, 1.0 / n),
                 reads=[sumsq_reg], writes=[exq_reg])
        S.op("dve", lambda e: e.scalar_tensor_tensor(
            nv_ap, m_ap, m_ap, exq_ap, OP.mult, OP.subtract),
            reads=[m_reg, exq_reg], writes=[nv_reg])
        S.op("act", lambda e: e.activation(lnv_ap, nv_ap, AF.Ln, bias=c_eps,
                                           scale=c_neg1),
             reads=[nv_reg, "cst0", "cst1"], writes=[lnv_reg])
        S.op("act", lambda e: e.activation(rstd_ap, lnv_ap, AF.Exp, scale=c_negh),
             reads=[lnv_reg, "cst2"], writes=[rstd_reg])
        S.op("dve", lambda e: e.scalar_tensor_tensor(
            bias_ap, m_ap, -1.0, rstd_ap, OP.mult, OP.mult),
            reads=[m_reg, rstd_reg], writes=[bias_reg])
        return m_ap, m_reg, rstd_ap, rstd_reg, bias_ap, bias_reg


    ln1_infos = {}

    def front(tc):
        pf = tc % 2
        s = str(pf)
        xs = sb["xs4"][:, (tc % 4) * D:(tc % 4) * D + D]
        xs_reg = ["xs%d" % (tc % 4), "xsh%d" % (tc % 4)]

        s4 = str(tc % 4)
        sum1_ap, sum1_reg = stc("sum1" + s4)
        ssq1_ap, ssq1_reg = stc("ssq1" + s4)
        S.op("act", lambda e, xs=xs, a=sum1_ap: e.activation(
            sb["sqj"][:, :], xs[:, :], AF.Copy, accum_out=a),
            reads=xs_reg, writes=["sqj", sum1_reg], est=1100)
        S.op("act", lambda e, xs=xs, a=ssq1_ap, s=s: e.activation(
            sb["sqj"][:, :], xs[:, :], AF.Square, accum_out=a),
            reads=xs_reg, writes=["sqj", ssq1_reg], est=1100)
        adj1 = sb["st"][:, ST["sum1" + s4]:ST["sum1" + s4] + 2]
        m1_ap, m1_reg, rstd1_ap, rstd1_reg, bias1_ap, bias1_reg = ln_scalars(
            "ln1" + s4, sum1_reg, ssq1_reg, sum1_ap, ssq1_ap, adj=adj1)
        S.op("act", lambda e, xs=xs, s=s, b=bias1_ap, sc=rstd1_ap: e.activation(
            sb["xn" + s][:, :], xs[:, :], AF.Identity, bias=b, scale=sc),
            reads=xs_reg + [rstd1_reg, bias1_reg], writes=["xn" + s], est=1100)
        ln1_infos[tc] = (sum1_ap, sum1_reg, m1_ap, m1_reg)

        S.op("sync", lambda e, s=s: e.dma_start(
            out=sb["xnf" + s].rearrange("p (c t) -> p c t", t=CK),
            in_=sb["xn" + s][:, :], transpose=True),
            reads=["xn" + s], writes=["xnf%s_h0" % s, "xnf" + s],
            dma=True, est=2500)
        xnf_regs = ["xnf" + s, "xnf%s_h0" % s]

        def em_a(e, s=s):
            last = None
            for kc in range(KC):
                last = e.matmul(pb[7][0:H, 0:CK],
                                sb["Wa"][:, kc * H:(kc + 1) * H],
                                sb["xnf" + s][:, kc * 128:(kc + 1) * 128],
                                start=(kc == 0), stop=(kc == KC - 1))
            return last
        S.op("pe", em_a, reads=xnf_regs + ["Wa", "pb7"], writes=["pb7"], est=500)

        S.op("dve", lambda e, s=s: e.tensor_scalar(
            sb["lu" + s][:, :], pb[7][0:H, 0:CK], sb["ba2"][:, :], None, OP.add),
            reads=["pb7", "ba2"], writes=["lu" + s])
        S.op("act", lambda e, s=s: e.activation(sb["le" + s][:, :],
                                                sb["lu" + s][:, :], AF.Abs),
             reads=["lu" + s], writes=["le" + s])
        S.op("act", lambda e, s=s: e.activation(sb["le" + s][:, :],
                                                sb["le" + s][:, :],
                                                AF.Exp, scale=sb["cst"][0:H, 1:2]),
             reads=["le" + s, "cst1"], writes=["le" + s])
        S.op("act", lambda e, s=s: e.activation(sb["le" + s][:, :],
                                                sb["le" + s][:, :],
                                                AF.Ln, bias=sb["cst"][0:H, 3:4]),
             reads=["le" + s, "cst3"], writes=["le" + s])
        S.op("dve", lambda e, s=s: e.scalar_tensor_tensor(
            sb["ll" + s][:, :], sb["lu" + s][:, :], 0.0, sb["le" + s][:, :],
            OP.min, OP.subtract),
            reads=["lu" + s, "le" + s], writes=["ll" + s])
        S.op("dve", lambda e, s=s: e.tensor_tensor_scan(
            sb["lcum" + s][:, :], sb["ll" + s][:, :], sb["zeros16"][:, :], 0.0,
            OP.add, OP.add),
            reads=["ll" + s, "zeros16"], writes=["lcum" + s])
        S.op("dve", lambda e, s=s: e.tensor_copy(
            sb["ext" + s][:, 0:2], sb["lcum" + s][:, 63:128:64]),
            reads=["lcum" + s], writes=["ext0" + s])
        S.op("dve", lambda e, s=s: e.tensor_sub(
            sb["ext" + s][:, 2:3], sb["ext" + s][:, 1:2],
            sb["ext" + s][:, 0:1]),
            reads=["ext0" + s], writes=["ext2" + s])
        S.op("act", lambda e, s=s: e.activation(
            sb["ext" + s][:, 0:3], sb["ext" + s][:, 0:3], AF.Exp),
            reads=["ext0" + s, "ext2" + s],
            writes=["ext0" + s, "ext2" + s])

        S.op("dve", lambda e, s=s: e.tensor_scalar(
            sb["karg" + s][:, :], sb["lcum" + s][:, :], sb["lcum" + s][:, 63:64],
            -1.0, OP.subtract, OP.mult),
            reads=["lcum" + s], writes=["karg" + s])
        S.op("dve", lambda e, s=s: e.tensor_scalar_min(
            sb["karg" + s][:, :], sb["karg" + s][:, :], 80.0),
            reads=["karg" + s], writes=["karg" + s])

        yield
        def proj_group(W, gidx, bank_i):
            def em(e, W=W, gidx=gidx, bank_i=bank_i, s=s):
                last = None
                for kc in range(KC):
                    last = e.matmul(
                        pb[bank_i][:, :],
                        sb["xnf" + s][:, kc * 128:(kc + 1) * 128],
                        sb[W][:, kc * 1024 + gidx * 512:kc * 1024 + (gidx + 1) * 512],
                        start=(kc == 0), stop=(kc == KC - 1))
                return last
            S.op("pe", em, reads=xnf_regs + [W, W + "h"],
                 writes=["pb%d" % bank_i], est=1900)

        def elu_evac_k(dst):
            S.op("act", lambda e: e.activation(
                sb["e_sb"][:, 0:512], pb[2][:, :], AF.Exp),
                reads=["pb2"], writes=["e_h0"], est=680)
            S.op("act", lambda e: e.activation(
                sb["r_sb"][:, 512:1024], pb[3][:, :], AF.Relu),
                reads=["pb3"], writes=["r_sb"], est=680)
            S.op("act", lambda e: e.activation(
                sb["e_sb"][:, 512:1024], pb[3][:, :], AF.Exp),
                reads=["pb3"], writes=["e_sb"], est=680)
            S.op("act", lambda e: e.activation(
                sb["r_sb"][:, 0:512], pb[2][:, :], AF.Relu),
                reads=["pb2"], writes=["r_h0"], est=680)
            S.op("dve", lambda e, dst=dst: e.tensor_scalar_min(
                sb[dst][:, :], sb["e_sb"][:, :], 1.0),
                reads=["e_sb", "e_h0"], writes=[dst], est=340)
            S.op("dve", lambda e, dst=dst: e.tensor_add(
                sb[dst][:, :], sb[dst][:, :], sb["r_sb"][:, :]),
                reads=[dst, "r_sb", "r_h0"], writes=[dst], est=620)

        def elu_evac(dst):
            S.op("act", lambda e: e.activation(
                sb["e_sb"][:, 0:512], pb[0][:, :], AF.Exp),
                reads=["pb0"], writes=["e_h0"], est=680)
            S.op("act", lambda e: e.activation(
                sb["r_sb"][:, 512:1024], pb[1][:, :], AF.Relu),
                reads=["pb1"], writes=["r_sb"], est=680)
            S.op("act", lambda e: e.activation(
                sb["e_sb"][:, 512:1024], pb[1][:, :], AF.Exp),
                reads=["pb1"], writes=["e_sb"], est=680)
            S.op("act", lambda e: e.activation(
                sb["r_sb"][:, 0:512], pb[0][:, :], AF.Relu),
                reads=["pb0"], writes=["r_h0"], est=680)
            S.op("dve", lambda e, dst=dst: e.tensor_scalar_min(
                sb[dst][:, :], sb["e_sb"][:, :], 1.0),
                reads=["e_sb", "e_h0"], writes=[dst], est=340)
            S.op("dve", lambda e, dst=dst: e.tensor_add(
                sb[dst][:, :], sb[dst][:, :], sb["r_sb"][:, :]),
                reads=[dst, "r_sb", "r_h0"], writes=[dst], est=620)

        proj_group("Wq", 0, 0)
        proj_group("Wq", 1, 1)
        proj_group("Wk", 0, 2)
        proj_group("Wk", 1, 3)
        elu_evac("qpre" + s)
        proj_group("Wv", 0, 0)
        proj_group("Wv", 1, 1)
        elu_evac_k("ktok" + s)
        out_ap = sb["vz" + s][:, :].rearrange(
            "p (h c) -> p h c", c=65)[:, :, 0:64]
        in_ap = pbs[0].rearrange("p (h c) -> p h c", c=64)
        S.op("act", lambda e, o=out_ap, i=in_ap: e.activation(o, i, AF.Copy),
             reads=["pb0", "pb1", "vz_ones%s" % s],
             writes=["vz" + s, "vz%s_h0" % s, "vzk" + s], est=1180)

        yield
        def em_rm(e, s=s):
            last = None
            for j in range(3):
                last = e.matmul(pb[7][0:1, 128 + j * 16:128 + (j + 1) * 16],
                                sb["ext" + s][:, j:j + 1], sb["i16"][:, :],
                                start=True, stop=True)
            return last
        S.op("pe", em_rm, reads=["ext0" + s, "ext2" + s, "i16", "pb7"],
             writes=["pb7"])
        S.op("act", lambda e, s=s: e.activation(sb["prow" + s][:, :],
                                                pb[7][0:1, 128:176], AF.Copy),
             reads=["pb7"], writes=["prow" + s])

        def em_pr(e, s=s):
            last = None
            for j in range(3):
                rr = sb["prow" + s][0:1, j * 16:(j + 1) * 16]
                last = e.matmul(pb[7][:, 176 + j * 8:176 + (j + 1) * 8],
                                sb["selhi"][:, :], rr[:, 0::2],
                                start=True, stop=False)
                last = e.matmul(pb[7][:, 176 + j * 8:176 + (j + 1) * 8],
                                sb["sello"][:, :], rr[:, 1::2],
                                start=False, stop=True)
            return last
        S.op("pe", em_pr, reads=["prow" + s, "selhi", "sello", "pb7"],
             writes=["pb7"])
        S.op("act", lambda e, s=s: e.activation(sb["rep" + s][:, :],
                                                pb[7][:, 176:200], AF.Copy),
             reads=["pb7"], writes=["rep" + s])

        S.op("pe", lambda e, s=s: e.matmul(pb[7][:, 200:216], sb["karg" + s][:, :],
                                           sb["i16"][:, :], start=True, stop=True),
             reads=["karg" + s, "i16", "pb7"], writes=["pb7"])
        S.op("act", lambda e, s=s: e.activation(sb["kapT" + s][:, :],
                                                pb[7][:, 200:216], AF.Exp),
             reads=["pb7"], writes=["kapT" + s])

        kexp = sb["kapT" + s].rearrange("p (h o) -> p h o", o=1).to_broadcast(
            [128, H, 64])
        vzv = sb["vz" + s][:, :].rearrange("p (h c) -> p h c", c=65)[:, :, 0:64]
        S.op("dve", lambda e, vzv=vzv, kexp=kexp: e.tensor_tensor(
            vzv, vzv, kexp, OP.mult),
            reads=["vz" + s, "vz%s_h0" % s, "kapT" + s],
            writes=["vzk" + s], est=1180)
        vzz = sb["vz" + s][:, :].rearrange("p (h c) -> p h c", c=65)[:, :, 64:65]
        S.op("dve", lambda e, vzz=vzz, s=s: e.tensor_copy(
            vzz, sb["kapT" + s].rearrange("p (h o) -> p h o", o=1)),
            reads=["kapT" + s, "vz_ones%s" % s],
            writes=["vzk" + s], est=120)

        yield
        S.op("sync", lambda e, s=s: e.dma_start(
            out=sb["kfm" + s].rearrange("p (c t) -> p c t", t=CK),
            in_=sb["ktok" + s][:, :], transpose=True),
            reads=["ktok" + s], writes=["kfm%s_h0" % s, "kfm" + s],
            dma=True, est=2500)
        for srcn, dst in (("qpre", "q"),):
            for half in range(2):
                def em_t(e, srcn=srcn, half=half, s=s):
                    last = None
                    for j in range(4):
                        p_ = half * 4 + j
                        last = e.matmul(pb[half][:, j * 128:(j + 1) * 128],
                                        sb[srcn + s][:, p_ * 128:(p_ + 1) * 128],
                                        sb["idb"][:, :], start=True, stop=True)
                    return last
                S.op("pe", em_t, reads=[srcn + s, "idb"], writes=["pb%d" % half], est=550)
            if dst == "k":
                pass
            else:
                def qout(half, side, s=s):
                    base = 1024 * half
                    v = sb["qAB" + s][side * 64:(side + 1) * 64,
                                      base:base + 1024]
                    return v.rearrange("p (q two c) -> p q two c", two=2,
                                       c=128)[:, :, side, :]
                def qin(half, side):
                    return pb[half][side * 64:(side + 1) * 64, :].rearrange(
                        "p (q c) -> p q c", c=128)
                S.op("dve", lambda e, o=qout(0, 0), i=qin(0, 0): e.tensor_copy(o, i),
                     reads=["pb0", "qAB%s_pad" % s], writes=["qAB%s_a0" % s], est=680)
                S.op("dve", lambda e, o=qout(0, 1), i=qin(0, 1): e.tensor_copy(o, i),
                     reads=["pb0", "qAB%s_pad" % s], writes=["qAB%s_b0" % s], est=680)
                S.op("act", lambda e, o=qout(1, 0), i=qin(1, 0): e.activation(
                    o, i, AF.Copy),
                    reads=["pb1", "qAB%s_pad" % s], writes=["qAB%s_a1" % s], est=680)
                S.op("act", lambda e, o=qout(1, 1), i=qin(1, 1): e.activation(
                    o, i, AF.Copy),
                    reads=["pb1", "qAB%s_pad" % s], writes=["qAB%s_b1" % s], est=680)


    def scan_chunk(tc):
        pf = tc % 2
        s = str(pf)
        cidx = tc // 2
        St = "S" + s
        qA = ["qAB%s_%s" % (s, t) for t in ("a0", "b0", "a1", "b1")] + \
            ["qAB%s_pad" % s]
        kf = ["kfm" + s, "kfm%s_h0" % s]
        vzr = ["vzk" + s]

        if cidx == 0:
            S.op("dve", lambda e, St=St: e.memset(sb[St][:, :], 0.0),
                 reads=[St], writes=[St])

        for w in range(2):
            for p in range(4 * w, 4 * w + 4):
                bank = 2 + (p % 4) // 2
                col = (p % 2) * 256

                def em_sc(e, p=p, bank=bank, col=col, s=s):
                    return e.matmul(pb[bank][:, col:col + 256],
                                    sb["kfm" + s][:, p * 128:(p + 1) * 128],
                                    sb["qAB" + s][:, p * 256:(p + 1) * 256],
                                    start=True, stop=True)
                S.op("pe", em_sc, reads=kf + qA, writes=["pb%d" % bank], est=220)
            for b in range(2):
                q = 2 * w + b
                S.op("dve", lambda e, b=b, q=q: e.tensor_tensor(
                    sb["scm"][:, q * 512:(q + 1) * 512], pb[2 + b][:, :],
                    sb["mask4"][:, :], OP.mult),
                    reads=["pb%d" % (2 + b), "mask4"], writes=["scm_b%d" % q],
                    est=680)

        eexp = sb["rep" + s][:, 0:8].rearrange(
            "p (q o) -> p q o", o=1).to_broadcast([128, NPAIR, 65])
        S.op("dve", lambda e, eexp=eexp, St=St: e.tensor_tensor(
            sb["Szb"][:, :].rearrange("p (q c) -> p q c", c=65),
            sb[St][:, :].rearrange("p (q c) -> p q c", c=65), eexp, OP.mult),
            reads=[St, "rep" + s], writes=["Szb"], est=650)

        for p in range(NPAIR):
            bank = 2 + p // 4
            col = (p % 4) * 65

            def em_P(e, p=p, bank=bank, col=col, s=s):
                hA, hB = 2 * p, 2 * p + 1
                e.matmul(pb[bank][0:64, col:col + 65],
                         sb["ktok" + s][:, hA * 64:(hA + 1) * 64],
                         sb["vz" + s][:, hA * 65:(hA + 1) * 65],
                         start=True, stop=True)
                return e.matmul(pb[bank][64:128, col:col + 65],
                                sb["ktok" + s][:, hB * 64:(hB + 1) * 64],
                                sb["vz" + s][:, hB * 65:(hB + 1) * 65],
                                start=True, stop=True, tile_position=(0, 64))
            S.op("pe", em_P, reads=["ktok" + s] + vzr, writes=["pb%d" % bank],
                 est=150)

        for bank, p0 in ((2, 0), (3, 4)):
            gexp = sb["rep" + s][:, 16 + p0:20 + p0].rearrange(
                "p (q o) -> p q o", o=1).to_broadcast([128, 4, 65])
            S.op("dve", lambda e, bank=bank, p0=p0, g=gexp: e.tensor_tensor(
                sb["Pg_all"][:, p0 * 65:(p0 + 4) * 65].rearrange(
                    "p (q c) -> p q c", c=65),
                pb[bank][:, 0:260].rearrange("p (q c) -> p q c", c=65), g, OP.mult),
                reads=["pb%d" % bank, "rep" + s], writes=["Pg_%d" % bank], est=340)
        NMB = (2, 2, 2, 3, 3, 3, 6, 6)
        for p in range(NPAIR):
            nm_bank = NMB[p]
            col = (p % 3) * 130

            def em_num(e, p=p, nm_bank=nm_bank, col=col, s=s):
                hA, hB = 2 * p, 2 * p + 1
                e.matmul(pb[nm_bank][:, col:col + 65],
                         sb["scm"][:, p * 256:p * 256 + 128],
                         sb["vz" + s][:, hA * 65:(hA + 1) * 65],
                         start=True, stop=False)
                e.matmul(pb[nm_bank][:, col:col + 65],
                         sb["qAB" + s][:, p * 256:p * 256 + 128],
                         sb["Szb"][:, p * 65:(p + 1) * 65], start=False, stop=True)
                e.matmul(pb[nm_bank][:, col + 65:col + 130],
                         sb["scm"][:, p * 256 + 128:p * 256 + 256],
                         sb["vz" + s][:, hB * 65:(hB + 1) * 65],
                         start=True, stop=False)
                return e.matmul(pb[nm_bank][:, col + 65:col + 130],
                                sb["qAB" + s][:, p * 256 + 128:(p + 1) * 256],
                                sb["Szb"][:, p * 65:(p + 1) * 65],
                                start=False, stop=True)
            S.op("pe", em_num,
                 reads=["scm_b%d" % (p // 2), "Szb"] + vzr + qA,
                 writes=["pb%d" % nm_bank], est=280)

        NP_BANK = ((2, 0, 3), (3, 3, 3), (6, 6, 2))
        for bank, p0, npr in NP_BANK:
            in_ap = pb[bank][:, 0:npr * 130].rearrange(
                "p (q x) -> p q x", x=130)[:, :, 64:130:65]
            out_ap = sb["den_all"][:, 2 * p0:2 * (p0 + npr)].rearrange(
                "p (a b) -> p a b", b=2)
            S.op("act", lambda e, o=out_ap, i=in_ap: e.activation(o, i, AF.Copy),
                 reads=["pb%d" % bank], writes=["den_b%d" % bank])
        S.op("dve", lambda e, s=s: e.scalar_tensor_tensor(
            sb["denf"][:, :], sb["kapT" + s][:, :], A_EPS, sb["den_all"][:, :],
            OP.mult, OP.add),
            reads=["kapT" + s, "den_b2", "den_b3", "den_b6"], writes=["denf"])
        S.op("dve", lambda e: e.reciprocal(sb["recip"][:, :], sb["denf"][:, :]),
             reads=["denf"], writes=["recip"])

        for bank, p0, npr in NP_BANK:
            for side in range(2):
                num_ap = pb[bank][:, 0:npr * 130].rearrange(
                    "p (q x) -> p q x", x=130)[:, :, side * 65:side * 65 + 64]
                rexp = sb["recip"][:, 2 * p0 + side:2 * (p0 + npr):2].rearrange(
                    "p (q o) -> p q o", o=1).to_broadcast([128, npr, 64])
                out_ap = sb["otok"][:, p0 * 128:(p0 + npr) * 128].rearrange(
                    "p (q t c) -> p q t c", t=2, c=64)[:, :, side, :]
                S.op("dve", lambda e, o=out_ap, i=num_ap, r=rexp: e.tensor_tensor(
                    o, i, r, OP.mult),
                    reads=["pb%d" % bank, "recip"],
                    writes=["ot_b%d_%d" % (bank, side)], est=340)

        aexp = sb["rep" + s][:, 8:16].rearrange(
            "p (q o) -> p q o", o=1).to_broadcast([128, NPAIR, 65])
        S.op("dve", lambda e, a=aexp, St=St: e.tensor_tensor(
            sb[St][:, :].rearrange("p (q c) -> p q c", c=65),
            sb[St][:, :].rearrange("p (q c) -> p q c", c=65), a, OP.mult),
            reads=[St, "rep" + s, "Szb"], writes=[St], est=650)
        S.op("dve", lambda e, St=St: e.tensor_add(sb[St][:, :], sb[St][:, :],
                                                  sb["Pg_all"][:, :]),
             reads=[St, "Pg_2", "Pg_3"], writes=[St], est=650)
        return ["ot_b%d_%d" % (b, sd) for b, _, _ in NP_BANK for sd in range(2)]

    def wo_chunk(j, tc, ot_regs, sq_cols, sum_cols):
        banks = (4, 5) if j == 0 else (2, 3)
        S.op("sync", lambda e: e.dma_start(
            out=sb["ofm"].rearrange("p (c t) -> p c t", t=CK),
            in_=sb["otok"][:, :], transpose=True),
            reads=ot_regs, writes=["ofm_h0", "ofm"], dma=True, est=2500)

        def em_wo(e, c0, c1, bank_i):
            last = None
            for kc in range(KC):
                last = e.matmul(pb[bank_i][:, 0:c1 - c0],
                                sb["ofm"][:, kc * 128:(kc + 1) * 128],
                                sb["Wo"][:, kc * 1025 + c0:kc * 1025 + c1],
                                start=(kc == 0), stop=(kc == KC - 1))
            return last
        S.op("pe", lambda e: em_wo(e, 0, 512, banks[0]),
             reads=["ofm", "ofm_h0", "Wo", "Woh"], writes=["pb%d" % banks[0]],
             est=1800)
        S.op("pe", lambda e: em_wo(e, 512, 1024, banks[1]),
             reads=["ofm", "ofm_h0", "Wo", "Woh"], writes=["pb%d" % banks[1]],
             est=1800)

        dst = sb["o_f"] if j == 0 else sb["ffn_f"][:, 0:D]
        rg = ("of_h0", "of_h1") if j == 0 else ("ofb_h0", "ofb_h1")
        ex = ["qt_f"] if j == 0 else ["ffn0_h0", "ffn0_h1"]
        sm0, sm0_reg = sum_cols
        sq0, sq0_reg = sq_cols
        pbsrc = pbs[banks[0] // 2]
        S.op("act", lambda e, dst=dst, a=sm0, pbsrc=pbsrc: e.activation(
            dst[:, :], pbsrc[:, :], AF.Copy, accum_out=a),
            reads=["pb%d" % banks[0], "pb%d" % banks[1]],
            writes=[rg[0], rg[1], sm0_reg] + ex, est=1155)
        S.op("act", lambda e, a=sq0, dst=dst: e.activation(
            sb["sqk"][:, :], dst[:, :], AF.Square, accum_out=a),
            reads=[rg[0], rg[1]], writes=["sqk", sq0_reg], est=1155)
        return (dst, [rg[0], rg[1]])

    def tail_pair(m):
        a, b = 2 * m, 2 * m + 1
        sa, sb_ = str(a % 2), str(b % 2)
        xsa = sb["xs4"][:, (a % 4) * D:(a % 4) * D + D]
        xsb = sb["xs4"][:, (b % 4) * D:(b % 4) * D + D]
        xsa_reg = ["xs%d" % (a % 4), "xsh%d" % (a % 4)]
        xsb_reg = ["xs%d" % (b % 4), "xsh%d" % (b % 4)]
        suma_ap, suma_reg, m1a_ap, m1a_reg = ln1_infos.pop(a)
        sumb_ap, sumb_reg, m1b_ap, m1b_reg = ln1_infos.pop(b)

        ota = scan_chunk(a)
        sqo_ap, sqo_reg = stc("sqo_a")
        sqo2_ap, sqo2_reg = stc("sqo_b")
        assert ST["sqo_b"] == ST["sqo_a"] + 1
        so_ap, so_reg = stc("sumo_a")
        so2_ap, so2_reg = stc("sumo_b")
        assert ST["sumo_b"] == ST["sumo_a"] + 1
        srcs_a = wo_chunk(0, a, ota, (sqo_ap, sqo_reg), (so_ap, so_reg))
        otb = scan_chunk(b)
        srcs_b = wo_chunk(1, b, otb, (sqo2_ap, sqo2_reg), (so2_ap, so2_reg))

        sopair = sb["st"][:, ST["sumo_a"]:ST["sumo_a"] + 2]
        ssqpair = sb["st"][:, ST["sqo_a"]:ST["sqo_a"] + 2]
        rstd2p, bias2p, ln2_regs = ln_pair("ln2", sopair, ssqpair,
                                           [so_reg, so2_reg],
                                           [sqo_reg, sqo2_reg])
        for j, ((src_ap, src_regs), xs_, xreg, col) in enumerate(
                ((srcs_a, xsa, xsa_reg, 0), (srcs_b, xsb, xsb_reg, 1))):
            S.op("act", lambda e, i=src_ap, col=col,
                 r2=rstd2p, b2=bias2p: e.activation(
                sb["x2d"][:, col * D:(col + 1) * D],
                i[:, :], AF.Identity, bias=b2[:, col:col + 1],
                scale=r2[:, col:col + 1]),
                reads=src_regs + ln2_regs,
                writes=["x2_%d" % col], est=1155)
            S.op("dve", lambda e, xs_=xs_, col=col: e.tensor_add(
                sb["x2d"][:, col * D:(col + 1) * D],
                sb["x2d"][:, col * D:(col + 1) * D], xs_[:, :]),
                reads=xreg, writes=["x2_%d" % col], est=1180)
            r0 = row0(a + col)
            dma("sync", y_h[r0:r0 + CK, :], sb["x2d"][:, col * D:(col + 1) * D],
                reads=["x2_%d" % col], writes=["yh%d" % (a + col)])

        for dd in (a + 4, b + 4):
            if dd < ntc:
                load_x(dd % 4, row0(dd))

        s3a_ap, s3a_reg = stc("ssq3_a")
        s3b_ap, s3b_reg = stc("ssq3_b")
        assert ST["ssq3_b"] == ST["ssq3_a"] + 1
        s3pair = sb["st"][:, ST["ssq3_a"]:ST["ssq3_a"] + 2]
        S.op("act", lambda e, a_=s3a_ap: e.activation(
            sb["sqk"][:, :], sb["x2d"][:, 0:D], AF.Square, accum_out=a_),
            reads=["x2_0"], writes=["sqk", s3a_reg], est=1100)
        S.op("act", lambda e, a_=s3b_ap: e.activation(
            sb["sqk"][:, :], sb["x2d"][:, D:2 * D], AF.Square, accum_out=a_),
            reads=["x2_1"], writes=["sqk", s3b_reg], est=1100)
        assert ST[("sum1" + str(a % 4))] + 1 == ST[("ssq1" + str(a % 4))]
        su3a_ap, su3a_reg = stc("sum3_a")
        su3b_ap, su3b_reg = stc("sum3_b")
        assert ST["sum3_b"] == ST["sum3_a"] + 1
        su3pair = sb["st"][:, ST["sum3_a"]:ST["sum3_a"] + 2]
        S.op("dve", lambda e, o=su3a_ap, i=suma_ap: e.tensor_copy(o, i),
             reads=[suma_reg], writes=[su3a_reg])
        S.op("dve", lambda e, o=su3b_ap, i=sumb_ap: e.tensor_copy(o, i),
             reads=[sumb_reg], writes=[su3b_reg])
        rstd3p, bias3p, ln3_regs = ln_pair("ln3", su3pair, s3pair,
                                           [su3a_reg, su3b_reg],
                                           [s3a_reg, s3b_reg])

        r4_ap = [stc("r4_%d" % i) for i in range(4)]
        assert ST["r4_3"] == ST["r4_0"] + 3
        r4 = sb["st"][:, ST["r4_0"]:ST["r4_0"] + 4]
        r4_regs = [r for _, r in r4_ap]
        r4v = r4.rearrange("p (c t) -> p c t", t=2)
        x2v = sb["x2d"][:, :].rearrange("p (c d) -> p c d", c=2)
        S.op("dve", lambda e, o=r4v[:, :, 0:1], i=x2v: e.tensor_reduce(
            o, i, AX.X, OP.max),
            reads=["x2_0", "x2_1"], writes=[r4_regs[0], r4_regs[2]], est=2300)
        S.op("dve", lambda e, o=r4v[:, :, 1:2], i=x2v: e.tensor_reduce(
            o, i, AX.X, OP.min),
            reads=["x2_0", "x2_1"], writes=[r4_regs[1], r4_regs[3]], est=2300)
        rstd_bc = rstd3p.rearrange("p (c o) -> p c o", o=1).to_broadcast([128, 2, 2])
        bias_bc = bias3p.rearrange("p (c o) -> p c o", o=1).to_broadcast([128, 2, 2])
        S.op("dve", lambda e, v=r4v, sc=rstd_bc: e.tensor_tensor(v, v, sc, OP.mult),
             reads=r4_regs + ln3_regs, writes=r4_regs)
        S.op("dve", lambda e, v=r4v, bc=bias_bc: e.tensor_tensor(v, v, bc, OP.add),
             reads=r4_regs + ln3_regs, writes=r4_regs)
        ta_ap, ta_reg = stc("amax_a")
        tb_ap, tb_reg = stc("amax_b")
        assert ST["amax_b"] == ST["amax_a"] + 1
        amaxp = sb["st"][:, ST["amax_a"]:ST["amax_a"] + 2]
        S.op("dve", lambda e, o=amaxp, v=r4v: e.reduce_max(
            o, v, AX.X, apply_absolute_value=True),
            reads=r4_regs, writes=[ta_reg, tb_reg])
        lsp_ap = [stc("ls_a"), stc("ls_b")]
        assert ST["ls_b"] == ST["ls_a"] + 1
        lspair = sb["st"][:, ST["ls_a"]:ST["ls_a"] + 2]
        ls_regs = [lsp_ap[0][1], lsp_ap[1][1]]
        S.op("dve", lambda e, o=lspair, i=amaxp: e.tensor_scalar(
            o, i, 1e-12, 1.0 / 127.0, OP.max, OP.mult),
            reads=[ta_reg, tb_reg], writes=ls_regs)
        rlsp_ap = [stc("rls_a"), stc("rls_b")]
        assert ST["rls_b"] == ST["rls_a"] + 1
        rlspair = sb["st"][:, ST["rls_a"]:ST["rls_a"] + 2]
        rls_regs = [rlsp_ap[0][1], rlsp_ap[1][1]]
        S.op("dve", lambda e, o=rlspair, i=lspair: e.reciprocal(o, i),
             reads=ls_regs, writes=rls_regs)
        sqs_ap = [stc("sqs_a"), stc("sqs_b")]
        sqspair = sb["st"][:, ST["sqs_a"]:ST["sqs_a"] + 2]
        sqs_regs = [sqs_ap[0][1], sqs_ap[1][1]]
        S.op("dve", lambda e, o=sqspair, r3=rstd3p, rl=rlspair: e.tensor_tensor(
            o, r3, rl, OP.mult),
            reads=ln3_regs + rls_regs, writes=sqs_regs)
        sqb_ap = [stc("sqb_a"), stc("sqb_b")]
        sqbpair = sb["st"][:, ST["sqb_a"]:ST["sqb_a"] + 2]
        sqb_regs = [sqb_ap[0][1], sqb_ap[1][1]]
        S.op("dve", lambda e, o=sqbpair, b3=bias3p, rl=rlspair: e.tensor_tensor(
            o, b3, rl, OP.mult),
            reads=ln3_regs + rls_regs, writes=sqb_regs)
        S.op("dve", lambda e, o=sqbpair: e.tensor_scalar_add(o, o, 0.5),
             reads=sqb_regs, writes=sqb_regs)

        facc_banks = []
        sfh = [stc("sfh_%d" % i) for i in range(2)]
        assert ST["sfh_1"] == ST["sfh_0"] + 1
        for col in range(2):
            x2c = sb["x2d"][:, col * D:(col + 1) * D]
            qtb = sb["wrep"]
            S.op("dve", lambda e, col=col, x2c=x2c, qtb=qtb: e.tensor_scalar(
                qtb[:, :], x2c,
                sqspair[:, col:col + 1], sqbpair[:, col:col + 1],
                OP.mult, OP.add),
                reads=["x2_%d" % col] + sqs_regs + sqb_regs,
                writes=["qtb", "wrep", "wrep_h0"],
                est=650)
            S.op("act", lambda e, qtb=qtb: e.activation(sb["qi"][:, :],
                                                        qtb[:, :], AF.Copy),
                 reads=["qtb"], writes=["qi"], est=1155)
            gqn = "gq" if col == 0 else "gq2"
            S.op("dve", lambda e, qtb=qtb, gqn=gqn: e.tensor_tensor(
                sb[gqn][:, :], sb["qi"][:, :], qtb[:, :], OP.is_gt),
                reads=["qi", "qtb"], writes=[gqn], est=1180)
            qln = "ql" if col == 0 else "ql2"
            S.op("dve", lambda e, qln=qln, gqn=gqn: e.scalar_tensor_tensor(
                sb[qln][:, :], sb[gqn][:, :], -1.0, sb["qi"][:, :],
                OP.mult, OP.add),
                reads=[gqn, "qi"], writes=[qln], est=1225)

            qb = (4, 5)
            qfn = "qlf" if col == 0 else "qlf2"
            S.op("sync", lambda e, qln=qln, qfn=qfn: e.dma_start(
                out=sb[qfn].rearrange("p (c t) -> p c t", t=CK),
                in_=sb[qln][:, :], transpose=True),
                reads=[qln], writes=[qfn + "_h0", qfn], dma=True, est=2500)

            def em_ffn(e, c0, c1, bank_i, qfn):
                last = None
                for kc in range(KC):
                    last = e.matmul(pb[bank_i][:, 0:c1 - c0],
                                    sb[qfn][:, kc * 128:(kc + 1) * 128],
                                    sb["QR"][:, kc * 1025 + c0:kc * 1025 + c1],
                                    start=(kc == 0), stop=(kc == KC - 1))
                return last
            qr_regs = ["QR_c%d" % c for c in range(KC)]
            S.op("pe", lambda e, qb=qb, qfn=qfn: em_ffn(e, 0, 512, qb[0], qfn),
                 reads=[qfn, qfn + "_h0"] + qr_regs, writes=["pb%d" % qb[0]],
                 est=1800)
            S.op("pe", lambda e, qb=qb, qfn=qfn: em_ffn(e, 512, 1024, qb[1], qfn),
                 reads=[qfn, qfn + "_h0"] + qr_regs, writes=["pb%d" % qb[1]],
                 est=1800)

            ex = ["ofb_h0", "ofb_h1"] if col == 0 else []
            sfh0, sfh0_reg = sfh[col]
            S.op("act", lambda e, col=col, a=sfh0: e.activation(
                sb["ffn_f"][:, col * D:(col + 1) * D], pbs[2][:, :], AF.Copy,
                accum_out=a),
                reads=["pb4", "pb5"],
                writes=["ffn%d_h0" % col, "ffn%d_h1" % col, sfh0_reg] + ex,
                est=1155)
            facc_banks.append(
                (sb["ffn_f"][:, col * D:(col + 1) * D],
                 ["ffn%d_h0" % col, "ffn%d_h1" % col]))

        sfpair = sb["st"][:, ST["sfh_0"]:ST["sfh_0"] + 2]
        sf_regs = [sfh[0][1], sfh[1][1]]
        S.op("dve", lambda e, o=sfpair, lsP=lspair: e.tensor_tensor(
            o, o, lsP, OP.mult),
            reads=sf_regs + ls_regs, writes=sf_regs)
        sqf_ap = [stc("ssqf_a"), stc("ssqf_b")]
        sqfpair = sb["st"][:, ST["ssqf_a"]:ST["ssqf_a"] + 2]
        sqf_regs = [sqf_ap[0][1], sqf_ap[1][1]]
        assert ST["ssqf_b"] == ST["ssqf_a"] + 1
        for col in range(2):
            src_ap, src_regs = facc_banks[col]
            S.op("act", lambda e, a_=sqf_ap[col][0], i=src_ap: e.activation(
                sb["sqk"][:, :], i[:, :], AF.Square, accum_out=a_),
                reads=src_regs, writes=["sqk", sqf_regs[col]], est=1155)
        S.op("dve", lambda e, o=sqfpair, lsP=lspair: e.tensor_tensor(
            o, o, lsP, OP.mult), reads=sqf_regs + ls_regs, writes=sqf_regs)
        S.op("dve", lambda e, o=sqfpair, lsP=lspair: e.tensor_tensor(
            o, o, lsP, OP.mult), reads=sqf_regs + ls_regs, writes=sqf_regs)
        rstd4p, bias4p, ln4_regs = ln_pair("ln4", sfpair, sqfpair, sf_regs,
                                           sqf_regs)
        sc4_ap = [stc("sc4_a"), stc("sc4_b")]
        sc4pair = sb["st"][:, ST["sc4_a"]:ST["sc4_a"] + 2]
        sc4_regs = [sc4_ap[0][1], sc4_ap[1][1]]
        S.op("dve", lambda e, o=sc4pair, lsP=lspair, r4P=rstd4p: e.tensor_tensor(
            o, lsP, r4P, OP.mult),
            reads=ls_regs + ln4_regs, writes=sc4_regs)
        for col in range(2):
            fbuf = sb["fn"] if col == 0 else sb["o_n"]
            fregs = ("fn_h0", "fn") if col == 0 else ("fnB_h0", "fnB")
            fex = [] if col == 0 else ["o_n"]
            src_ap, src_regs = facc_banks[col]
            S.op("act", lambda e, i=src_ap, col=col, fbuf=fbuf,
                 b4=bias4p, s4=sc4pair: e.activation(
                fbuf[:, :], i[:, :],
                AF.Identity, bias=b4[:, col:col + 1],
                scale=s4[:, col:col + 1]),
                reads=src_regs + ln4_regs + sc4_regs,
                writes=[fregs[0], fregs[1]] + fex, est=1155)
            r0 = row0(a + col)
            S.op("pool", lambda e, r0=r0, fbuf=fbuf: e.dma_start(
                out=y_h[r0:r0 + CK, :], in_=fbuf[:, :], accum_op=OP.add),
                reads=[fregs[0], fregs[1], "yh%d" % (a + col)],
                writes=["yh%d" % (a + col)], dma=True, est=2500)

    def ln_pair(tag, sum2, ssq2, sum_regs, ssq_regs, n=D):
        m2_ap = [stc(tag + "_m_a"), stc(tag + "_m_b")]
        m2 = sb["st"][:, ST[tag + "_m_a"]:ST[tag + "_m_a"] + 2]
        m_regs = [m2_ap[0][1], m2_ap[1][1]]
        nv_ap = [stc(tag + "_nv_a"), stc(tag + "_nv_b")]
        nv2 = sb["st"][:, ST[tag + "_nv_a"]:ST[tag + "_nv_a"] + 2]
        nv_regs = [nv_ap[0][1], nv_ap[1][1]]
        r_ap = [stc(tag + "_r_a"), stc(tag + "_r_b")]
        r2 = sb["st"][:, ST[tag + "_r_a"]:ST[tag + "_r_a"] + 2]
        r_regs = [r_ap[0][1], r_ap[1][1]]
        b_ap = [stc(tag + "_b_a"), stc(tag + "_b_b")]
        b2 = sb["st"][:, ST[tag + "_b_a"]:ST[tag + "_b_a"] + 2]
        b_regs = [b_ap[0][1], b_ap[1][1]]
        ex_ap = [stc(tag + "_e_a"), stc(tag + "_e_b")]
        ex2 = sb["st"][:, ST[tag + "_e_a"]:ST[tag + "_e_a"] + 2]
        ex_regs = [ex_ap[0][1], ex_ap[1][1]]
        S.op("dve", lambda e: e.tensor_scalar_mul(m2, sum2, 1.0 / n),
             reads=sum_regs, writes=m_regs)
        S.op("dve", lambda e: e.tensor_scalar_mul(ex2, ssq2, 1.0 / n),
             reads=ssq_regs, writes=ex_regs)
        S.op("dve", lambda e: e.tensor_tensor(nv2, m2, m2, OP.mult),
             reads=m_regs, writes=nv_regs)
        S.op("dve", lambda e: e.tensor_tensor(nv2, nv2, ex2, OP.subtract),
             reads=nv_regs + ex_regs, writes=nv_regs)
        S.op("act", lambda e: e.activation(nv2, nv2, AF.Ln, bias=c_eps,
                                           scale=c_neg1),
             reads=nv_regs + ["cst0", "cst1"], writes=nv_regs)
        S.op("act", lambda e: e.activation(r2, nv2, AF.Exp, scale=c_negh),
             reads=nv_regs + ["cst2"], writes=r_regs)
        S.op("dve", lambda e: e.scalar_tensor_tensor(
            b2, m2, -1.0, r2, OP.mult, OP.mult),
            reads=m_regs + r_regs, writes=b_regs)
        return r2, b2, r_regs + b_regs

    dbg_h = {}
    if debug_tc is not None:
        for name, _regs in DBG_TENSORS:
            ap = sb[name]
            dbg_h[name] = nc.dram_tensor("dbg_" + name, list(ap.shape), ap.dtype,
                                         kind="ExternalOutput").ap()

    def run_all(g):
        for _ in g:
            pass

    assert ntc % 2 == 0 and spc % 2 == 0
    run_all(front(0))
    run_all(front(1))
    for m in range(ntc // 2):
        tail_pair(m)
        if 2 * m + 2 < ntc:
            run_all(front(2 * m + 2))
        if 2 * m + 3 < ntc:
            run_all(front(2 * m + 3))

    return nc, S


def make_consts():
    idb = np.eye(128, dtype=np.float32).astype(ml_dtypes.bfloat16)
    idf = np.eye(128, dtype=np.float32)
    i16 = np.eye(H, dtype=np.float32)
    m = (np.arange(128)[None, :] >= np.arange(128)[:, None]).astype(np.float32)
    mask4 = np.concatenate([m, m, m, m], axis=1).astype(ml_dtypes.bfloat16)
    selhi = np.zeros((1, 128), dtype=np.float32)
    selhi[0, 0:64] = 1.0
    sello = np.zeros((1, 128), dtype=np.float32)
    sello[0, 64:128] = 1.0
    ones1 = np.ones((1, 128), dtype=np.float32)
    return {"idb": idb, "idf": idf, "i16": i16, "mask4": mask4,
            "selhi": selhi, "sello": sello, "ones1": ones1}


def build(nseq, spc, trunc=None, debug_tc=None):
    nc, S = build_program(nseq, spc, debug_tc=debug_tc)
    final = {"sync": [("qs%d" % i, None) for i in range(NQ)],
             "pool": [("qg%d" % i, None) for i in range(NQ)]}
    return _emit_scheduled(nc, S, extra_final=final, trunc=trunc)


def kernel(L0_attn_x, L0_attn_Wq, L0_attn_Wk, L0_attn_Wv, L0_attn_Wo,
           L0_attn_Wa, L0_attn_ba, L0_ffn_rhs):
    x = np.ascontiguousarray(np.asarray(L0_attn_x, dtype=np.float32))
    B, N, _ = x.shape
    nseq, spc = 2, N // CK
    nc = build(nseq, spc)
    common = {
        "wq": np.ascontiguousarray(L0_attn_Wq, dtype=np.float32),
        "wk": np.ascontiguousarray(L0_attn_Wk, dtype=np.float32),
        "wv": np.ascontiguousarray(L0_attn_Wv, dtype=np.float32),
        "wo": np.ascontiguousarray(L0_attn_Wo, dtype=np.float32),
        "wa": np.ascontiguousarray(L0_attn_Wa, dtype=np.float32),
        "ba": np.ascontiguousarray(L0_attn_ba, dtype=np.float32),
        "rhs": np.ascontiguousarray(L0_ffn_rhs, dtype=np.float32),
    }
    common.update(make_consts())
    in_maps = []
    for c in range(8):
        m = dict(common)
        m["x"] = np.ascontiguousarray(x[2 * c:2 * c + 2].reshape(nseq * N, D))
        in_maps.append(m)
    res = run_bass_kernel_spmd(nc, in_maps, core_ids=list(range(8)))
    out = np.empty((B, N, D), dtype=np.float32)
    for c in range(8):
        out[2 * c:2 * c + 2] = res.results[c]["y"].reshape(2, N, D)
    return out
